# version 10 (final)
# baseline (speedup 1.0000x reference)
import numpy as np
import ml_dtypes
import concourse.bass as bass
import concourse.mybir as mybir
from concourse.bass_utils import run_bass_kernel_spmd
from contextlib import ExitStack

F32 = mybir.dt.float32
BF16 = mybir.dt.bfloat16
AF = mybir.ActivationFunctionType
ALU = mybir.AluOpType
AX = mybir.AxisListType
BFd = ml_dtypes.bfloat16

B, T, D = 4, 4096, 2048
H, HKV, HD = 32, 8, 64
E, FF = 16, 768
NSL = T // 128
NQT = 16
S_STRUCT = [32, 28, 16, 12]
EXC_STRUCT = [list(range(24, 32)), list(range(20, 28)),
              list(range(8, 16)), list(range(4, 12))]
DIAG_STRUCT = [sorted(set(range(28, 32)) | set(range(20, 24))),
               sorted(set(range(24, 28)) | set(range(16, 20))),
               sorted(set(range(4, 8)) | set(range(12, 16))),
               sorted(set(range(0, 4)) | set(range(8, 12)))]
TILE_STARTS = {0: [3584, 3072, 512, 0], 1: [2560, 2048, 1536, 1024]}
DEBUG = False


class _MI:
    def then_inc(self, *a, **k):
        return self


class _ME:

    def __init__(self, name):
        self.engine = "mock." + name

    def __getattr__(self, item):
        return lambda *a, **k: _MI()


class Sch:

    def __init__(self):
        self.cnt, self.h, self.lastw = {}, {}, {}

    def reset(self):
        for k in self.cnt:
            self.cnt[k] = 0
        self.lastw.clear()

    def add(self, name, handle):
        self.h[name] = handle
        self.cnt[name] = 0

    def inc(self, name, instr, n=1):
        instr.then_inc(self.h[name], n)
        self.cnt[name] += n
        return (name, self.cnt[name])

    def tok(self, name):
        return (name, self.cnt[name])

    def wait(self, eng, *toks):
        key = str(eng.engine)
        for t in toks:
            if t is None:
                continue
            nm, v = t
            if v <= 0:
                continue
            if self.lastw.get((key, nm), 0) < v:
                eng.wait_ge(self.h[nm], v)
                self.lastw[(key, nm)] = v


def build_nc():
    nc = bass.Bass("TRN2", target_bir_lowering=False)

    x = nc.dram_tensor("x", [T, D], F32, kind="ExternalInput")
    xq = nc.dram_tensor("xq", [2048, D], F32, kind="ExternalInput")
    wk_r = nc.dram_tensor("wk_r", [128, 16, 512], F32, kind="ExternalInput")
    wv_r = nc.dram_tensor("wv_r", [128, 16, 512], F32, kind="ExternalInput")
    wq_r = nc.dram_tensor("wq_r", [4, 128, 16, 512], F32, kind="ExternalInput")
    wo_r = nc.dram_tensor("wo_r", [4, 128, 16, 512], F32, kind="ExternalInput")
    rt_r = nc.dram_tensor("rt_r", [128, 16, 16], F32, kind="ExternalInput")
    gate_r = nc.dram_tensor("gate_r", [E, 6, 128, 16, 128], F32, kind="ExternalInput")
    up_r = nc.dram_tensor("up_r", [E, 6, 128, 16, 128], F32, kind="ExternalInput")
    down_r = nc.dram_tensor("down_r", [E, 768, 2048], F32, kind="ExternalInput")
    cosk = nc.dram_tensor("cosk", [T, 256], F32, kind="ExternalInput")
    sink = nc.dram_tensor("sink", [T, 256], F32, kind="ExternalInput")
    cosq = nc.dram_tensor("cosq", [2048, 1024], F32, kind="ExternalInput")
    sinq = nc.dram_tensor("sinq", [2048, 1024], F32, kind="ExternalInput")
    identb = nc.dram_tensor("identb", [128, 128], BF16, kind="ExternalInput")
    identf = nc.dram_tensor("identf", [128, 128], F32, kind="ExternalInput")
    maskd = nc.dram_tensor("maskd", [4, 8, 128, 1024], BF16, kind="ExternalInput")
    excrow = nc.dram_tensor("excrow", [4, 128], F32, kind="ExternalInput")
    out = nc.dram_tensor("out", [2048, D], F32, kind="ExternalOutput")
    if DEBUG:
        dbg_xn = nc.dram_tensor("dbg_xn", [128, 16 * 2048], BF16, kind="ExternalOutput")
        dbg_q = nc.dram_tensor("dbg_q", [16, 128, 2048], BF16, kind="ExternalOutput")
        dbg_o = nc.dram_tensor("dbg_o", [16, 128, 2048], BF16, kind="ExternalOutput")
        dbg_cb = nc.dram_tensor("dbg_cb", [16, 2048], BF16, kind="ExternalOutput")
        dbg_bias = nc.dram_tensor("dbg_bias", [128, 4 * 128 + 4 * 16 + 1], F32, kind="ExternalOutput")

    q_spill = nc.dram_tensor("q_spill", [16, 128, 2048], BF16, kind="Internal")
    o_spill = nc.dram_tensor("o_spill", [16, 128, 2048], BF16, kind="Internal")
    xn_spill = nc.dram_tensor("xn_spill", [128, 16 * 2048], BF16, kind="Internal")
    z_spill = nc.dram_tensor("z_spill", [32, 2048], BF16, kind="Internal")

    S = Sch()
    ctx = ExitStack()
    for nm in ["ld", "lw", "la", "sp", "mm", "ve", "sc", "gp", "so"]:
        S.add(nm, ctx.enter_context(nc.semaphore(nm)))

    _alloc_cache = {}

    def SB(st, n, sh, dt):
        if n not in _alloc_cache:
            _alloc_cache[n] = st.enter_context(nc.sbuf_tensor(n, sh, dt))
        return _alloc_cache[n]

    def PS(st, n, sh, dt):
        if n not in _alloc_cache:
            _alloc_cache[n] = st.enter_context(nc.psum_tensor(n, sh, dt))
        return _alloc_cache[n]

    idb = SB(ctx, "idb", [128, 128], BF16)
    idf = SB(ctx, "idf", [128, 128], F32)
    ones_b = SB(ctx, "ones_b", [128, 128], BF16)
    c_eps = SB(ctx, "c_eps", [128, 1], F32)
    c_s8 = SB(ctx, "c_s8", [128, 1], F32)
    c_nh = SB(ctx, "c_nh", [128, 1], F32)
    c_hl = SB(ctx, "c_hl", [128, 1], F32)
    c_lb = SB(ctx, "c_lb", [128, 1], F32)
    comb_sb = SB(ctx, "comb_sb", [16, 2048], BF16)
    knmax = SB(ctx, "knmax", [128, 8], F32)
    knm2 = SB(ctx, "knm2", [128, 1], F32)
    knsc_bf = SB(ctx, "knsc_bf", [128, 1], BF16)
    negp = SB(ctx, "negp", [16, 4], F32)
    negrow = SB(ctx, "negrow", [128, 4 * 16], F32)
    bias_p = SB(ctx, "bias_p", [128, 4 * 16], F32)
    bias_x = SB(ctx, "bias_x", [128, 4 * 128], F32)
    qn2 = SB(ctx, "qn2", [128, 16 * 32], F32)
    qn2p = SB(ctx, "qn2p", [128, 16 * 16], F32)
    attn_ctx = ExitStack()
    K_cmp = SB(attn_ctx, "K_cmp", [128, 4, T], BF16)
    V_sb = SB(attn_ctx, "V_sb", [128, NSL, 520], BF16)

    def barrier_toks():
        return [S.tok(n) for n in ["ld", "lw", "la", "sp", "mm", "ve", "sc", "gp"]]

    _mocks = {n: _ME(n) for n in ["sync", "vector", "scalar", "tensor", "gpsimd"]}

    def _emit(cur, real_eng):
        S.reset()

        class _B:
            def __getattr__(self, name):
                def run(f):
                    f(real_eng if name == cur else _mocks[name])
                return run
        blk = _B()
        def c0(vector):
            vector.memset(c_eps[:, :], 1e-5)
            vector.memset(c_s8[:, :], 0.125)
            vector.memset(c_nh[:, :], -0.5)
            vector.memset(c_hl[:, :], 0.5)
            vector.memset(c_lb[:, :], float(np.log(1.01 / 8.0)))
            vector.memset(ones_b[:, :], 1.0)
            vector.memset(knmax[:, :], 0.0)
            vg = V_sb[:, :, :].rearrange("p c (g y) -> p (c g) y", y=65)
            S.inc("ve", vector.memset(vg[:, :, 64:65], 1.0))
        blk.vector(c0)
        t_consts = S.tok("ve")

        def c1(sync):
            S.inc("ld", sync.dma_start(out=idb[:, :], in_=identb[:, :]), 16)
            S.inc("ld", sync.dma_start(out=idf[:, :], in_=identf[:, :]), 16)
        blk.sync(c1)
        t_ident = S.tok("ld")

        actx = ExitStack()
        wk_bf = SB(actx, "wk_bf", [128, 16 * 512], BF16)
        wv_bf = SB(actx, "wv_bf", [128, 16 * 512], BF16)
        x_sl = [SB(actx, f"x_sl{i}", [128, D], F32) for i in range(4)]
        xn_sl = [SB(actx, f"xn_sl{i}", [128, D], BF16) for i in range(4)]
        xnf_sl = [SB(actx, f"xnf_sl{i}", [128, 16 * 128], BF16) for i in range(4)]
        k_tm = [SB(actx, f"k_tm{i}", [128, 512], F32) for i in range(4)]
        krot = [SB(actx, f"krot{i}", [128, 512], BF16) for i in range(4)]
        rtab = [SB(actx, f"rtab{i}", [128, 512], F32) for i in range(4)]
        sqa = SB(actx, "sqa", [128, 512], F32)
        st_ln_r = [SB(actx, f"st_ln{i}", [128, 32], F32) for i in range(4)]
        kn2 = SB(actx, "kn2", [128, 8], F32)
        rtv = [SB(actx, f"rtv{i}", [128, 256], F32) for i in range(2)]
        rtg = [SB(actx, f"rtg{i}", [128, 256], F32) for i in range(2)]

        tr_ps = [PS(actx, f"tr_ps{i}", [128, 1024], BF16) for i in range(2)]
        K_ps = [PS(actx, f"K_ps{i}", [128, 512], F32) for i in range(2)]
        V_ps = [PS(actx, f"V_ps{i}", [128, 512], F32) for i in range(2)]
        ktr_ps = [PS(actx, f"ktr_ps{i}", [128, 512], BF16) for i in range(2)]

        def a_w(gpsimd):
            S.inc("lw", gpsimd.dma_start(out=wk_bf[:, :].rearrange("p (k f) -> p k f", k=16),
                                         in_=wk_r[:, :, :]), 16)
            S.inc("lw", gpsimd.dma_start(out=wv_bf[:, :].rearrange("p (k f) -> p k f", k=16),
                                         in_=wv_r[:, :, :]), 16)
        blk.gpsimd(a_w)
        t_wkv = S.tok("lw")

        tk_xload = [None] * NSL
        tk_xn = [None] * NSL
        tk_xnf = [None] * NSL
        tk_kv = [None] * NSL
        tk_kevac = [None] * NSL
        tk_krot = [None] * NSL
        tk_rtab = [None] * NSL
        tk_ktr = [None] * NSL
        tk_kcmp = [None] * NSL
        tk_vevac = [None] * NSL
        tk_kn = [None] * NSL
        tk_knsq = [None] * NSL
        tk_ropev = [None] * NSL
        tk_sum = [None] * NSL
        tk_rstd = [None] * NSL
        tk_tr = [None] * NSL

        def st_ln(s):
            b = s % 4

            def a_load(sync, s=s, b=b):
                if s >= 4:
                    S.wait(sync, tk_xn[s - 4])
                tk_xload[s] = S.inc("ld", sync.dma_start(
                    out=x_sl[b][:, :], in_=x[s * 128:(s + 1) * 128, :]), 16)
                if s >= 4:
                    S.wait(sync, tk_krot[s - 4], tk_ropev[s - 4])
                S.inc("ld", sync.dma_start(out=rtab[b][:, 0:256],
                                           in_=cosk[s * 128:(s + 1) * 128, :]), 16)
                tk_rtab[s] = S.inc("ld", sync.dma_start(
                    out=rtab[b][:, 256:512], in_=sink[s * 128:(s + 1) * 128, :]), 16)
            blk.sync(a_load)

            def a_bn(vector, s=s, b=b):
                S.wait(vector, tk_xload[s])
                if s >= 4:
                    S.wait(vector, tk_xn[s - 4])
                st_lnx = st_ln_r[b]
                for q4 in range(4):
                    S.inc("ve", vector.bn_stats(st_lnx[:, 8 + q4 * 6:8 + (q4 + 1) * 6],
                                                x_sl[b][:, q4 * 512:(q4 + 1) * 512]))
                S.wait(vector, S.tok("ve"))
                S.inc("ve", vector.bn_aggr(st_lnx[:, 2:4], st_lnx[:, 8:32]))
            blk.vector(a_bn)
            tv_var = S.tok("ve")

            def a_rstd(scalar, s=s, b=b):
                S.wait(scalar, tv_var, t_consts)
                st_lnx = st_ln_r[b]
                S.inc("sc", scalar.activation(st_lnx[:, 3:4], st_lnx[:, 3:4], AF.Ln, bias=c_eps[:, 0:1]))
                S.wait(scalar, S.tok("sc"))
                tk_rstd[s] = S.inc("sc", scalar.activation(st_lnx[:, 3:4], st_lnx[:, 3:4],
                                                           AF.Exp, scale=c_nh[:, 0:1]))
            blk.scalar(a_rstd)

        def st_x(s):
            b = s % 4

            def a_xn(vector, s=s, b=b):
                S.wait(vector, tk_rstd[s])
                if s >= 4:
                    S.wait(vector, tk_xnf[s - 4])
                st_lnx = st_ln_r[b]
                tk_xn[s] = S.inc("ve", vector.tensor_scalar(
                    xn_sl[b][:, :], x_sl[b][:, :], st_lnx[:, 2:3], st_lnx[:, 3:4],
                    op0=ALU.subtract, op1=ALU.mult))
            blk.vector(a_xn)

            def a_tr(tensor, s=s, b=b):
                S.wait(tensor, tk_xn[s], t_ident)
                if s >= 1:
                    S.wait(tensor, tk_xnf[s - 1])
                for g in range(2):
                    last = None
                    for i in range(8):
                        k = g * 8 + i
                        last = tensor.transpose(tr_ps[g][:, i * 128:(i + 1) * 128],
                                                xn_sl[b][:, k * 128:(k + 1) * 128], idb[:, :])
                    S.inc("mm", last)
                tk_tr[s] = S.tok("mm")
            blk.tensor(a_tr)

        def st_kv(s):
            b = s % 4

            def a_trev(vector, s=s, b=b):
                if s >= 4:
                    S.wait(vector, tk_kv[s - 4])
                S.wait(vector, ("mm", tk_tr[s][1] - 1))
                S.inc("ve", vector.tensor_copy(xnf_sl[b][:, 0:1024], tr_ps[0][:, :]))
                S.wait(vector, tk_tr[s])
                tk_xnf[s] = S.inc("ve", vector.tensor_copy(xnf_sl[b][:, 1024:2048], tr_ps[1][:, :]))
            blk.vector(a_trev)

            def a_kv(tensor, s=s, b=b):
                S.wait(tensor, tk_xnf[s], t_wkv)
                if s >= 2:
                    S.wait(tensor, tk_kevac[s - 2], tk_vevac[s - 2])
                last = None
                for k in range(16):
                    lhs = xnf_sl[b][:, k * 128:(k + 1) * 128]
                    tensor.matmul(K_ps[s % 2][:, :], lhs, wk_bf[:, k * 512:(k + 1) * 512],
                                  start=(k == 0), stop=(k == 15))
                    last = tensor.matmul(V_ps[s % 2][:, :], lhs, wv_bf[:, k * 512:(k + 1) * 512],
                                         start=(k == 0), stop=(k == 15))
                tk_kv[s] = S.inc("mm", last)
            blk.tensor(a_kv)

        def st_k1(s):
            b = s % 4

            def a_kev(vector, s=s, b=b):
                S.wait(vector, tk_kv[s])
                if s >= 4:
                    S.wait(vector, tk_krot[s - 4], tk_ropev[s - 4])
                tk_kevac[s] = S.inc("ve", vector.tensor_copy(k_tm[b][:, :], K_ps[s % 2][:, :]))
            blk.vector(a_kev)

            def a_vev(vector, s=s, b=b):
                S.wait(vector, tk_kv[s], t_consts)
                vg = V_sb[:, s, :].rearrange("p (g y) -> p g y", y=65)
                tk_vevac[s] = S.inc("ve", vector.tensor_copy(
                    vg[:, :, 0:64], V_ps[s % 2][:, :].rearrange("p (g y) -> p g y", y=64)))
            blk.vector(a_vev)

            def a_rg(gpsimd, s=s, b=b):
                S.wait(gpsimd, tk_kevac[s], tk_rtab[s])
                if s >= 4:
                    S.wait(gpsimd, tk_ktr[s - 4], tk_knsq[s - 4])
                kt = k_tm[b][:, :].rearrange("p (h i two) -> p h i two", two=2, i=32)
                kr = krot[b][:, :].rearrange("p (h i two) -> p h i two", two=2, i=32)
                cosv = rtab[b][:, 0:256].rearrange("p (h i) -> p h i", i=32)
                sinv = rtab[b][:, 256:512].rearrange("p (h i) -> p h i", i=32)
                r0 = rtv[0][:, :].rearrange("p (h i) -> p h i", i=32)
                r1 = rtv[1][:, :].rearrange("p (h i) -> p h i", i=32)
                g0 = rtg[0][:, :].rearrange("p (h i) -> p h i", i=32)
                g1 = rtg[1][:, :].rearrange("p (h i) -> p h i", i=32)
                S.inc("gp", gpsimd.tensor_tensor(r0, kt[:, :, :, 1], sinv, ALU.mult))
                S.inc("gp", gpsimd.tensor_tensor(r1, kt[:, :, :, 0], cosv, ALU.mult))
                S.inc("gp", gpsimd.tensor_tensor(g0, kt[:, :, :, 0], sinv, ALU.mult))
                S.inc("gp", gpsimd.tensor_tensor(g1, kt[:, :, :, 1], cosv, ALU.mult))
                S.wait(gpsimd, S.tok("gp"))
                S.inc("gp", gpsimd.tensor_tensor(kr[:, :, :, 0], r1, r0, ALU.subtract))
                tk_krot[s] = S.inc("gp", gpsimd.tensor_tensor(kr[:, :, :, 1], g0, g1, ALU.add))
                tk_ropev[s] = tk_krot[s]
            blk.gpsimd(a_rg)

        def st_k2(s):
            b = s % 4

            def a_ktr(tensor, s=s, b=b):
                S.wait(tensor, tk_krot[s], tk_ropev[s])
                if s >= 2:
                    S.wait(tensor, tk_kcmp[s - 2])
                last = None
                for t4 in range(4):
                    last = tensor.transpose(ktr_ps[s % 2][:, t4 * 128:(t4 + 1) * 128],
                                            krot[b][:, t4 * 128:(t4 + 1) * 128], idb[:, :])
                tk_ktr[s] = S.inc("mm", last)
            blk.tensor(a_ktr)

            def a_kcm(vector, s=s, b=b):
                S.wait(vector, tk_ktr[s])
                tk_kcmp[s] = S.inc("ve", vector.tensor_copy(
                    K_cmp[:, :, s * 128:(s + 1) * 128],
                    ktr_ps[s % 2][:, :].rearrange("p (t c) -> p t c", t=4)))
            blk.vector(a_kcm)

            def a_knsq(gpsimd, s=s, b=b):
                S.wait(gpsimd, tk_krot[s])
                if s >= 1:
                    S.wait(gpsimd, tk_kn[s - 1])
                tk_knsq[s] = S.inc("gp", gpsimd.tensor_tensor(sqa[:, :], krot[b][:, :],
                                                              krot[b][:, :], ALU.mult))
            blk.gpsimd(a_knsq)

            def a_kn(vector, s=s):
                S.wait(vector, tk_knsq[s])
                S.inc("ve", vector.reduce_sum(kn2[:, :], sqa[:, :]
                                              .rearrange("p (g y) -> p g y", y=64), axis=AX.X))
                S.wait(vector, S.tok("ve"))
                tk_kn[s] = S.inc("ve", vector.tensor_max(knmax[:, :], knmax[:, :], kn2[:, :]))
            blk.vector(a_kn)

        for it_a in range(NSL + 4):
            if 4 <= it_a and it_a - 4 < NSL:
                st_k2(it_a - 4)
            if 3 <= it_a and it_a - 3 < NSL:
                st_k1(it_a - 3)
            if 2 <= it_a and it_a - 2 < NSL:
                st_kv(it_a - 2)
            if 1 <= it_a and it_a - 1 < NSL:
                st_x(it_a - 1)
            if it_a < NSL:
                st_ln(it_a)

        def a_kf1(vector):
            S.wait(vector, tk_kn[NSL - 1])
            S.inc("ve", vector.reduce_max(kn2[:, 0:1], knmax[:, :], axis=AX.X))
        blk.vector(a_kf1)
        tv1 = S.tok("ve")

        def a_kf2(tensor):
            S.wait(tensor, tv1)
            S.inc("mm", tensor.transpose(K_ps[0][0:1, 0:128], kn2[:, 0:1], idf[:, :]))
        blk.tensor(a_kf2)
        tm1 = S.tok("mm")

        def a_kf3(vector):
            S.wait(vector, tm1)
            S.inc("ve", vector.reduce_max(st_ln_r[0][0:1, 0:1], K_ps[0][0:1, 0:128], axis=AX.X))
            S.wait(vector, S.tok("ve"))
            S.inc("ve", vector.tensor_copy(knsc_bf[0:1, 0:1], st_ln_r[0][0:1, 0:1]))
        blk.vector(a_kf3)
        tv2 = S.tok("ve")

        def a_kf4(tensor):
            S.wait(tensor, tv2)
            S.inc("mm", tensor.matmul(K_ps[1][:, 0:1], ones_b[0:1, 0:128], knsc_bf[0:1, 0:1],
                                      start=True, stop=True))
        blk.tensor(a_kf4)
        tm2 = S.tok("mm")

        def a_kf5(vector):
            S.wait(vector, tm2)
            S.inc("ve", vector.tensor_copy(knm2[:, :], K_ps[1][:, 0:1]))
        blk.vector(a_kf5)

        tA_bar = barrier_toks()
        actx.close()

        bctx = ExitStack()
        xn_q = SB(bctx, "xn_q", [128, 16 * 2048], BF16)
        sqb = SB(bctx, "sqb", [128, 512], F32)

        def wait_bar(eng, bar):
            S.wait(eng, *bar)

        b1 = ExitStack()
        x_slB = [SB(b1, f"x_slB{i}", [128, D], F32) for i in range(4)]
        xn_slB = [SB(b1, f"xn_slB{i}", [128, D], BF16) for i in range(4)]
        st_lnB_r = [SB(b1, f"st_lnB{i}", [128, 32], F32) for i in range(4)]
        tr_psB = [PS(b1, f"tr_psB{i}", [128, 1024], BF16) for i in range(2)]
        tkB_x = [None] * NQT
        tkB_xn = [None] * NQT
        tkB_q = [None] * NQT
        for s in range(NQT):
            b = s % 4

            def b_load(sync, s=s, b=b):
                wait_bar(sync, tA_bar)
                if s >= 4:
                    S.wait(sync, tkB_xn[s - 4])
                tkB_x[s] = S.inc("ld", sync.dma_start(
                    out=x_slB[b][:, :], in_=xq[s * 128:(s + 1) * 128, :]), 16)
            blk.sync(b_load)

            def b_bn(vector, s=s, b=b):
                wait_bar(vector, tA_bar)
                S.wait(vector, tkB_x[s])
                if s >= 4:
                    S.wait(vector, tkB_xn[s - 4])
                st_lnB = st_lnB_r[b]
                for q4 in range(4):
                    S.inc("ve", vector.bn_stats(st_lnB[:, 8 + q4 * 6:8 + (q4 + 1) * 6],
                                                x_slB[b][:, q4 * 512:(q4 + 1) * 512]))
                S.wait(vector, S.tok("ve"))
                S.inc("ve", vector.bn_aggr(st_lnB[:, 2:4], st_lnB[:, 8:32]))
            blk.vector(b_bn)
            tv_var = S.tok("ve")

            def b_rstd(scalar, s=s, b=b):
                S.wait(scalar, tv_var)
                st_lnB = st_lnB_r[b]
                S.inc("sc", scalar.activation(st_lnB[:, 3:4], st_lnB[:, 3:4], AF.Ln, bias=c_eps[:, 0:1]))
                S.wait(scalar, S.tok("sc"))
                S.inc("sc", scalar.activation(st_lnB[:, 3:4], st_lnB[:, 3:4], AF.Exp, scale=c_nh[:, 0:1]))
            blk.scalar(b_rstd)
            ts_rstd = S.tok("sc")

            def b_xn(vector, s=s, b=b):
                S.wait(vector, ts_rstd)
                if s >= 4:
                    S.wait(vector, tkB_q[s - 4])
                tkB_xn[s] = S.inc("ve", vector.tensor_scalar(
                    xn_slB[b][:, :], x_slB[b][:, :], st_lnB_r[b][:, 2:3], st_lnB_r[b][:, 3:4],
                    op0=ALU.subtract, op1=ALU.mult))
            blk.vector(b_xn)

            def b_tr(tensor, s=s, b=b):
                wait_bar(tensor, tA_bar)
                S.wait(tensor, tkB_xn[s], t_ident)
                if s >= 1:
                    S.wait(tensor, tkB_q[s - 1])
                for g in range(2):
                    last = None
                    for i in range(8):
                        k = g * 8 + i
                        last = tensor.transpose(tr_psB[g][:, i * 128:(i + 1) * 128],
                                                xn_slB[b][:, k * 128:(k + 1) * 128], idb[:, :])
                    S.inc("mm", last)
            blk.tensor(b_tr)
            t_tr = S.tok("mm")

            def b_trev(vector, s=s):
                S.wait(vector, ("mm", t_tr[1] - 1))
                xqv = xn_q[:, :].rearrange("p (k c) -> p k c", k=16)
                S.inc("ve", vector.tensor_copy(xqv[:, 0:8, s * 128:(s + 1) * 128],
                                               tr_psB[0][:, :].rearrange("p (i c) -> p i c", i=8)))
                S.wait(vector, t_tr)
                tkB_q[s] = S.inc("ve", vector.tensor_copy(
                    xqv[:, 8:16, s * 128:(s + 1) * 128],
                    tr_psB[1][:, :].rearrange("p (i c) -> p i c", i=8)))
            blk.vector(b_trev)
        t_xnq = tkB_q[NQT - 1]
        tB1_bar = barrier_toks()
        b1.close()

        b2 = ExitStack()
        wq_bf = SB(b2, "wq_bf", [128, 16 * 512], BF16)
        rt_bf = SB(b2, "rt_bf", [128, 256], BF16)
        q_tm = [SB(b2, f"q_tm{i}", [128, 512], F32) for i in range(4)]
        qrot = [SB(b2, f"qrot{i}", [128, 512], BF16) for i in range(4)]
        rtabB = [SB(b2, f"rtabB{i}", [128, 512], F32) for i in range(4)]
        rtvB = [SB(b2, f"rtvB{i}", [128, 256], F32) for i in range(2)]
        rtgB = [SB(b2, f"rtgB{i}", [128, 256], F32) for i in range(2)]
        qsp_st = [SB(b2, f"qsp_st{i}", [128, 512], BF16) for i in range(4)]
        r_ex = SB(b2, "r_ex", [128, 16], F32)
        r_st = SB(b2, "r_st", [128, 8], F32)
        r_ge = SB(b2, "r_ge", [128, 16], F32)
        r_cb = SB(b2, "r_cb", [128, 16], BF16)
        exc_f = SB(b2, "exc_f", [128, 4 * 128], F32)
        excb_bf = SB(b2, "excb_bf", [128, 4 * 128], BF16)
        nrow_bf = SB(b2, "nrow_bf", [128, 4 * 16], BF16)

        Q_ps = [PS(b2, f"Q_ps{i}", [128, 512], F32) for i in range(2)]
        R_ps = PS(b2, "R_ps", [128, 16], F32)
        qtr_ps = [PS(b2, f"qtr_ps{i}", [128, 512], BF16) for i in range(2)]
        cb_psB = PS(b2, "cb_psB", [16, 128], BF16)
        nm_ps = PS(b2, "nm_ps", [16, 128], F32)
        bc_psB = PS(b2, "bc_psB", [128, 512], F32)

        def b_rt(gpsimd):
            wait_bar(gpsimd, tB1_bar)
            S.inc("lw", gpsimd.dma_start(out=rt_bf[:, :].rearrange("p (k e) -> p k e", k=16),
                                         in_=rt_r[:, :, :]), 16)
        blk.gpsimd(b_rt)
        t_rt = S.tok("lw")

        tk_wq = [None] * 4
        tk_qmm = [[None] * NQT for _ in range(4)]
        tk_qev = [[None] * NQT for _ in range(4)]
        tk_qrot = [[None] * NQT for _ in range(4)]
        tk_qn = [[None] * NQT for _ in range(4)]
        tk_qtr = [[None] * NQT for _ in range(4)]
        tk_qst = [[None] * NQT for _ in range(4)]
        tk_qsp = [[None] * NQT for _ in range(4)]
        tk_rtb = [[None] * NQT for _ in range(4)]
        tk_qrv = [[None] * NQT for _ in range(4)]
        tk_qsqd = [[None] * NQT for _ in range(4)]
        tk_comb = [None] * NQT
        tk_rmm = [None] * NQT
        tk_rex = [None] * NQT

        for ct in range(4):
            def b_wq(gpsimd, ct=ct):
                wait_bar(gpsimd, tB1_bar)
                if ct >= 1:
                    S.wait(gpsimd, tk_qmm[ct - 1][NQT - 1])
                tk_wq[ct] = S.inc("lw", gpsimd.dma_start(
                    out=wq_bf[:, :].rearrange("p (k f) -> p k f", k=16), in_=wq_r[ct, :, :, :]), 16)
            blk.gpsimd(b_wq)

            def b2_stage1(ct, s):
                b = (ct * NQT + s) % 4
                it = ct * NQT + s
                def b_qmm(tensor, ct=ct, s=s, b=b, it=it):
                    S.wait(tensor, tk_wq[ct], t_xnq)
                    if ct == 0:
                        S.wait(tensor, t_rt)
                        if s >= 1:
                            S.wait(tensor, tk_rex[s - 1])
                    if it >= 2:
                        pct, ps = divmod(it - 2, NQT)
                        S.wait(tensor, tk_qev[pct][ps])
                    xqv = xn_q[:, :].rearrange("p (k c) -> p k c", k=16)
                    last = lr = None
                    for k in range(16):
                        last = tensor.matmul(Q_ps[it % 2][:, :], xqv[:, k, s * 128:(s + 1) * 128],
                                             wq_bf[:, k * 512:(k + 1) * 512],
                                             start=(k == 0), stop=(k == 15))
                        if ct == 0:
                            lr = tensor.matmul(R_ps[:, :], xqv[:, k, s * 128:(s + 1) * 128],
                                               rt_bf[:, k * 16:(k + 1) * 16],
                                               start=(k == 0), stop=(k == 15))
                    if ct == 0:
                        tk_rmm[s] = S.inc("mm", lr)
                    tk_qmm[ct][s] = S.inc("mm", last)
                blk.tensor(b_qmm)

                def b_qev(vector, ct=ct, s=s, b=b, it=it):
                    S.wait(vector, tk_qmm[ct][s])
                    if it >= 4:
                        pct, ps = divmod(it - 4, NQT)
                        S.wait(vector, tk_qrot[pct][ps], tk_qn[pct][ps], tk_qrv[pct][ps])
                    tk_qev[ct][s] = S.inc("ve", vector.tensor_copy(q_tm[b][:, :], Q_ps[it % 2][:, :]))
                blk.vector(b_qev)

                def b_rtabf(sync, ct=ct, s=s, b=b, it=it):
                    wait_bar(sync, tB1_bar)
                    if it >= 4:
                        pct, ps = divmod(it - 4, NQT)
                        S.wait(sync, tk_qrot[pct][ps], tk_qrv[pct][ps])
                    S.inc("ld", sync.dma_start(
                        out=rtabB[b][:, 0:256],
                        in_=cosq[s * 128:(s + 1) * 128, ct * 256:(ct + 1) * 256]), 16)
                    tk_rtb[ct][s] = S.inc("ld", sync.dma_start(
                        out=rtabB[b][:, 256:512],
                        in_=sinq[s * 128:(s + 1) * 128, ct * 256:(ct + 1) * 256]), 16)
                blk.sync(b_rtabf)

                def b_rv(vector, ct=ct, s=s, b=b, it=it):
                    S.wait(vector, tk_qev[ct][s], tk_rtb[ct][s])
                    if it >= 4:
                        pct, ps = divmod(it - 4, NQT)
                        S.wait(vector, tk_qtr[pct][ps], tk_qsqd[pct][ps])
                    qt = q_tm[b][:, :].rearrange("p (h i two) -> p h i two", two=2, i=32)
                    qr = qrot[b][:, :].rearrange("p (h i two) -> p h i two", two=2, i=32)
                    cosv = rtabB[b][:, 0:256].rearrange("p (h i) -> p h i", i=32)
                    sinv = rtabB[b][:, 256:512].rearrange("p (h i) -> p h i", i=32)
                    r0 = rtvB[0][:, :].rearrange("p (h i) -> p h i", i=32)
                    r1 = rtvB[1][:, :].rearrange("p (h i) -> p h i", i=32)
                    S.inc("ve", vector.tensor_tensor(r0, qt[:, :, :, 1], sinv, ALU.mult))
                    S.inc("ve", vector.tensor_tensor(r1, qt[:, :, :, 0], cosv, ALU.mult))
                    S.wait(vector, S.tok("ve"))
                    S.inc("ve", vector.tensor_tensor(qr[:, :, :, 0], r1, r0, ALU.subtract))
                blk.vector(b_rv)
                tv_re2 = S.tok("ve")
                tk_qrv[ct][s] = tv_re2

                def b_rg(gpsimd, ct=ct, s=s, b=b, it=it):
                    S.wait(gpsimd, tk_qev[ct][s], tk_rtb[ct][s])
                    if it >= 4:
                        pct, ps = divmod(it - 4, NQT)
                        S.wait(gpsimd, tk_qtr[pct][ps], tk_qsqd[pct][ps])
                    qt = q_tm[b][:, :].rearrange("p (h i two) -> p h i two", two=2, i=32)
                    qr = qrot[b][:, :].rearrange("p (h i two) -> p h i two", two=2, i=32)
                    cosv = rtabB[b][:, 0:256].rearrange("p (h i) -> p h i", i=32)
                    sinv = rtabB[b][:, 256:512].rearrange("p (h i) -> p h i", i=32)
                    g0 = rtgB[0][:, :].rearrange("p (h i) -> p h i", i=32)
                    g1 = rtgB[1][:, :].rearrange("p (h i) -> p h i", i=32)
                    S.inc("gp", gpsimd.tensor_tensor(g0, qt[:, :, :, 0], sinv, ALU.mult))
                    S.inc("gp", gpsimd.tensor_tensor(g1, qt[:, :, :, 1], cosv, ALU.mult))
                    S.wait(gpsimd, S.tok("gp"))
                    tk_qrot[ct][s] = S.inc("gp", gpsimd.tensor_tensor(qr[:, :, :, 1], g0, g1, ALU.add))
                blk.gpsimd(b_rg)

                if ct == 0:
                    def b_r1(vector, s=s):
                        S.wait(vector, tk_rmm[s])
                        S.inc("ve", vector.reduce_max(r_st[:, 0:1], R_ps[:, :], axis=AX.X, negate=True))
                    blk.vector(b_r1)
                    tv_rm = S.tok("ve")

                    def b_r2(scalar, s=s):
                        S.wait(scalar, tv_rm)
                        tk_rex[s] = S.inc("sc", scalar.activation(r_ex[:, :], R_ps[:, :], AF.Exp,
                                                                  bias=r_st[:, 0:1]))
                    blk.scalar(b_r2)

                    def b_r3(vector, s=s):
                        S.wait(vector, tk_rex[s])
                        S.inc("ve", vector.reduce_max(r_st[:, 1:2], r_ex[:, :], axis=AX.X))
                        S.wait(vector, S.tok("ve"))
                        S.inc("ve", vector.tensor_single_scalar(r_ge[:, :], r_ex[:, :],
                                                                r_st[:, 1:2], ALU.is_equal))
                        S.wait(vector, S.tok("ve"))
                        S.inc("ve", vector.scalar_tensor_tensor(r_ge[:, :], r_ge[:, :], -1e30,
                                                                r_ex[:, :], op0=ALU.mult, op1=ALU.add))
                        S.wait(vector, S.tok("ve"))
                        S.inc("ve", vector.reduce_max(r_st[:, 2:3], r_ge[:, :], axis=AX.X))
                        S.wait(vector, S.tok("ve"))
                        S.inc("ve", vector.tensor_single_scalar(r_ge[:, :], r_ex[:, :],
                                                                r_st[:, 2:3], ALU.is_ge))
                        S.inc("ve", vector.tensor_tensor(r_st[:, 3:4], r_st[:, 1:2],
                                                         r_st[:, 2:3], ALU.add))
                        S.wait(vector, S.tok("ve"))
                        S.inc("ve", vector.reciprocal(r_st[:, 3:4], r_st[:, 3:4]))
                        S.wait(vector, S.tok("ve"))
                        S.inc("ve", vector.scalar_tensor_tensor(r_cb[:, :], r_ex[:, :], r_st[:, 3:4],
                                                                r_ge[:, :], op0=ALU.mult, op1=ALU.mult))
                    blk.vector(b_r3)
                    tv_cb = S.tok("ve")

                    def b_r4(tensor, s=s):
                        S.wait(tensor, tv_cb)
                        if s >= 1:
                            S.wait(tensor, tk_comb[s - 1])
                        S.inc("mm", tensor.transpose(cb_psB[:, :], r_cb[:, :], idb[:, :]))
                    blk.tensor(b_r4)
                    tm_cb = S.tok("mm")

                    def b_r5(vector, s=s):
                        S.wait(vector, tm_cb)
                        tk_comb[s] = S.inc("ve", vector.tensor_copy(
                            comb_sb[:, s * 128:(s + 1) * 128], cb_psB[:, :]))
                    blk.vector(b_r5)


            def b2_stage2(ct, s):
                b = (ct * NQT + s) % 4
                it = ct * NQT + s
                def b_qsq(vector, ct=ct, s=s, b=b, it=it):
                    S.wait(vector, tk_qrot[ct][s], tk_qrv[ct][s])
                    if it >= 1:
                        pct, ps = divmod(it - 1, NQT)
                        S.wait(vector, tk_qn[pct][ps])
                    tk_qsqd[ct][s] = S.inc("ve", vector.tensor_tensor(
                        sqb[:, :], qrot[b][:, :], qrot[b][:, :], ALU.mult))
                blk.vector(b_qsq)
                ts_qsq = S.tok("ve")

                def b_qn(vector, ct=ct, s=s):
                    S.wait(vector, ts_qsq)
                    tk_qn[ct][s] = S.inc("ve", vector.reduce_sum(
                        qn2[:, s * 32 + ct * 8:s * 32 + (ct + 1) * 8],
                        sqb[:, :].rearrange("p (g y) -> p g y", y=64), axis=AX.X))
                blk.vector(b_qn)

                def b_qtr(tensor, ct=ct, s=s, b=b, it=it):
                    S.wait(tensor, tk_qrot[ct][s], tk_qrv[ct][s])
                    if it >= 2:
                        pct, ps = divmod(it - 2, NQT)
                        S.wait(tensor, tk_qst[pct][ps])
                    last = None
                    for t4 in range(4):
                        last = tensor.transpose(qtr_ps[it % 2][:, t4 * 128:(t4 + 1) * 128],
                                                qrot[b][:, t4 * 128:(t4 + 1) * 128], idb[:, :])
                    tk_qtr[ct][s] = S.inc("mm", last)
                blk.tensor(b_qtr)

                def b_qst(vector, ct=ct, s=s, b=b, it=it):
                    S.wait(vector, tk_qtr[ct][s])
                    if it >= 4:
                        pct, ps = divmod(it - 4, NQT)
                        S.wait(vector, tk_qsp[pct][ps])
                    tk_qst[ct][s] = S.inc("ve", vector.tensor_copy(qsp_st[b][:, :], qtr_ps[it % 2][:, :]))
                blk.vector(b_qst)

                def b_qsp(gpsimd, ct=ct, s=s, b=b):
                    S.wait(gpsimd, tk_qst[ct][s])
                    for t4 in range(4):
                        for half64 in range(2):
                            h = ct * 8 + 2 * t4 + half64
                            p_new = (h // 8) * 4 + (h % 4)
                            hi = 0 if (h % 8) < 4 else 1
                            tk_qsp[ct][s] = S.inc("sp", gpsimd.dma_start(
                                out=q_spill[p_new, hi * 64:hi * 64 + 64, s * 128:(s + 1) * 128],
                                in_=qsp_st[b][half64 * 64:half64 * 64 + 64,
                                              t4 * 128:(t4 + 1) * 128]), 16)
                blk.gpsimd(b_qsp)


            for s in range(NQT):
                it0 = ct * NQT + s
                if it0 >= 2:
                    p2 = it0 - 2
                    b2_stage2(p2 // NQT, p2 % NQT)
                b2_stage1(ct, s)
            if ct == 3:
                for tail in [NQT * 4 - 2, NQT * 4 - 1]:
                    b2_stage2(tail // NQT, tail % NQT)

        tk_negrow = [None] * 4
        for j in range(4):
            def b_p1(vector, j=j):
                for s4 in range(4):
                    s = j * 4 + s4
                    S.wait(vector, tk_qn[3][s])
                    S.inc("ve", vector.reduce_sum(
                        qn2p[:, s * 16:(s + 1) * 16].rearrange("p (blk i) -> p blk i", i=4),
                        qn2[:, s * 32:(s + 1) * 32]
                        .rearrange("p (blk two i) -> p blk i two", two=2, i=4),
                        axis=AX.X))
            blk.vector(b_p1)

            for s4 in range(4):
                s = j * 4 + s4

                def b_p2(tensor, j=j, s=s):
                    S.wait(tensor, S.tok("ve"))
                    S.inc("mm", tensor.transpose(nm_ps[:, :], qn2p[:, s * 16:(s + 1) * 16], idf[:, :]))
                blk.tensor(b_p2)
                tmn = S.tok("mm")

                def b_p3(vector, j=j, s4=s4, tmn=tmn):
                    S.wait(vector, tmn)
                    if s4 == 0:
                        S.inc("ve", vector.reduce_max(negp[:, j:j + 1], nm_ps[:, :], axis=AX.X))
                    else:
                        S.inc("ve", vector.reduce_max(r_st[0:16, 4:5], nm_ps[:, :], axis=AX.X))
                        S.wait(vector, S.tok("ve"))
                        S.inc("ve", vector.tensor_max(negp[:, j:j + 1], negp[:, j:j + 1],
                                                      r_st[0:16, 4:5]))
                    S.wait(vector, S.tok("ve"))
                blk.vector(b_p3)

            def b_c8a(vector, j=j):
                S.wait(vector, S.tok("ve"))
                S.inc("ve", vector.tensor_tensor(negp[:, j:j + 1], negp[:, j:j + 1],
                                                 knm2[0:16, 0:1], ALU.mult))
            blk.vector(b_c8a)
            tva = S.tok("ve")

            def b_c8b(scalar, j=j):
                S.wait(scalar, tva)
                S.inc("sc", scalar.activation(negp[:, j:j + 1], negp[:, j:j + 1], AF.Ln))
                S.wait(scalar, S.tok("sc"))
                S.inc("sc", scalar.activation(negp[:, j:j + 1], negp[:, j:j + 1], AF.Exp,
                                              scale=c_hl[0:16, 0:1], bias=c_lb[0:16, 0:1]))
            blk.scalar(b_c8b)
            tsb = S.tok("sc")

            def b_c8c(vector, j=j):
                S.wait(vector, tsb)
                S.inc("ve", vector.tensor_scalar(negp[:, j:j + 1], negp[:, j:j + 1], -1.0, None, ALU.mult))
            blk.vector(b_c8c)
            tvn = S.tok("ve")

            def b_c8d(tensor, j=j):
                S.wait(tensor, tvn)
                S.inc("mm", tensor.transpose(nm_ps[0:1, 0:16], negp[0:16, j:j + 1], idf[0:16, 0:16]))
            blk.tensor(b_c8d)
            tmt = S.tok("mm")

            def b_c8e(vector, j=j):
                S.wait(vector, tmt)
                tk_negrow[j] = S.inc("ve", vector.tensor_copy(negrow[0:1, j * 16:(j + 1) * 16],
                                                              nm_ps[0:1, 0:16]))
            blk.vector(b_c8e)

        def b_xl(sync):
            wait_bar(sync, tB1_bar)
            S.inc("ld", sync.dma_start(out=exc_f[0:4, 0:128], in_=excrow[:, :]), 16)
        blk.sync(b_xl)
        t_excd = S.tok("ld")

        def b_xm(gpsimd):
            S.wait(gpsimd, t_excd)
            for j in range(1, 4):
                S.inc("sp", gpsimd.dma_start(out=exc_f[0:1, j * 128:(j + 1) * 128],
                                             in_=exc_f[j:j + 1, 0:128]), 16)
        blk.gpsimd(b_xm)
        t_excm = S.tok("sp")

        def b_bx(vector):
            for j in range(4):
                S.wait(vector, tk_negrow[j])
            S.wait(vector, t_excm)
            for j in range(4):
                for c8i in range(8):
                    S.inc("ve", vector.tensor_copy(
                        bias_x[0:1, j * 128 + c8i * 16:j * 128 + (c8i + 1) * 16],
                        negrow[0:1, j * 16:(j + 1) * 16]))
            S.wait(vector, S.tok("ve"))
            S.inc("ve", vector.tensor_tensor(bias_x[0:1, :], bias_x[0:1, :], exc_f[0:1, :], ALU.add))
            S.wait(vector, S.tok("ve"))
            S.inc("ve", vector.tensor_copy(excb_bf[0:1, :], bias_x[0:1, :]))
            S.inc("ve", vector.tensor_copy(nrow_bf[0:1, :], negrow[0:1, :]))
        blk.vector(b_bx)
        tv_rows = S.tok("ve")

        def b_bc1(tensor):
            S.wait(tensor, tv_rows)
            S.inc("mm", tensor.matmul(bc_psB[:, 0:64], ones_b[0:1, 0:128], nrow_bf[0:1, :],
                                      start=True, stop=True))
        blk.tensor(b_bc1)
        tm_bc1 = S.tok("mm")

        def b_bcev1(vector):
            S.wait(vector, tm_bc1)
            S.inc("ve", vector.tensor_copy(bias_p[:, :], bc_psB[:, 0:64]))
        blk.vector(b_bcev1)
        tv_bp = S.tok("ve")

        def b_bc2(tensor):
            S.wait(tensor, tv_bp)
            S.inc("mm", tensor.matmul(bc_psB[:, :], ones_b[0:1, 0:128], excb_bf[0:1, :],
                                      start=True, stop=True))
        blk.tensor(b_bc2)
        tm_bc2 = S.tok("mm")

        def b_bcev2(vector):
            S.wait(vector, tm_bc2)
            S.inc("ve", vector.tensor_copy(bias_x[:, :], bc_psB[:, :]))
        blk.vector(b_bcev2)
        tv_bias = S.tok("ve")

        def b_xnsp(gpsimd):
            S.wait(gpsimd, t_xnq)
            S.inc("sp", gpsimd.dma_start(out=xn_spill[:, :], in_=xn_q[:, :]), 16)
        blk.gpsimd(b_xnsp)
        t_xnsp = S.tok("sp")

        tB_bar = barrier_toks()
        b2.close()
        bctx.close()

        cctx = ExitStack()
        Qt = [SB(cctx, f"Qt{i}", [128, 16, 512], BF16) for i in range(2)]
        P_sb = [SB(cctx, f"P_sb{i}", [128, 1024], BF16) for i in range(4)]
        mask_sb = SB(cctx, "mask_sb", [128, 8, 1024], BF16)
        o_st = [SB(cctx, f"o_st{i}", [65, 1024], BF16) for i in range(2)]

        sc_ps = [PS(cctx, f"sc_ps{i}", [128, 1024], F32) for i in range(3)]
        O_psA = PS(cctx, "O_psA", [65, 512], F32)
        O_psB = PS(cctx, "O_psB", [65, 512], F32)

        t_kdup = None

        tk_qt = [None] * 4
        tk_qt_use = [None] * 4
        tk_msk = [None] * 4
        tk_msk_use = [None] * 4
        prev_odone = [None]
        prev_osp = [None]

        for j in range(4):
            Sj = S_STRUCT[j]
            excset = set(EXC_STRUCT[j])
            diagmap = {c: i for i, c in enumerate(DIAG_STRUCT[j])}

            def c_qt(scalar, j=j):
                wait_bar(scalar, tB_bar)
                if j >= 2:
                    S.wait(scalar, tk_qt_use[j - 2])
                for hp in range(16):
                    tk_qt[j] = S.inc("la", scalar.dma_start(
                        out=Qt[j % 2][:, hp, :],
                        in_=q_spill[hp, :, j * 512:(j + 1) * 512]), 16)
            blk.scalar(c_qt)

            def c_msk(sync, j=j):
                wait_bar(sync, tB_bar)
                if j >= 1:
                    S.wait(sync, tk_msk_use[j - 1])
                tk_msk[j] = S.inc("ld", sync.dma_start(
                    out=mask_sb[:, :, :],
                    in_=maskd[j, :, :, :].rearrange("m p c -> p m c")), 16)
            blk.sync(c_msk)

            for hp in range(16):
                kt = hp // 4
                gA, gB = 2 * kt, 2 * kt + 1
                tk_sc = {}
                tk_exp = {}
                tk_mask = {}
                tk_av = {}

                LAG = 2
                for it in range(Sj + LAG):
                    if it < Sj:
                        ci = it
                        pb = ci % 3
                        pp = ci % 4

                        def c_score(tensor, j=j, hp=hp, ci=ci, pb=pb, kt=kt):
                            S.wait(tensor, tk_qt[j], tv_bias)
                            if ci >= 3:
                                S.wait(tensor, tk_exp[ci - 3])
                            tensor.matmul(sc_ps[pb][:, 0:512],
                                          K_cmp[0:64, kt, ci * 128:(ci + 1) * 128],
                                          Qt[j % 2][0:64, hp, :], start=True, stop=True)
                            lB = tensor.matmul(sc_ps[pb][:, 512:1024],
                                               K_cmp[64:128, kt, ci * 128:(ci + 1) * 128],
                                               Qt[j % 2][64:128, hp, :], start=True, stop=True)
                            tk_sc[ci] = S.inc("mm", lB)
                        blk.tensor(c_score)

                        def c_exp(scalar, j=j, hp=hp, ci=ci, pb=pb, pp=pp):
                            S.wait(scalar, tk_sc[ci], t_consts)
                            if ci >= 4:
                                S.wait(scalar, tk_av[ci - 4])
                            if ci in excset:
                                cidx = EXC_STRUCT[j].index(ci)
                                bias_ap = bias_x[:, j * 128 + cidx * 16 + hp:j * 128 + cidx * 16 + hp + 1]
                            else:
                                bias_ap = bias_p[:, j * 16 + hp:j * 16 + hp + 1]
                            tk_exp[ci] = S.inc("sc", scalar.activation(
                                P_sb[pp][:, :], sc_ps[pb][:, :], AF.Exp,
                                bias=bias_ap, scale=c_s8[:, 0:1]))
                        blk.scalar(c_exp)

                        if ci in diagmap:
                            def c_mask(vector, j=j, ci=ci, pp=pp, mi=diagmap[ci]):
                                S.wait(vector, tk_exp[ci], tk_msk[j])
                                tk_mask[ci] = S.inc("ve", vector.tensor_tensor(
                                    P_sb[pp][:, :], P_sb[pp][:, :], mask_sb[:, mi, :], ALU.mult))
                            blk.vector(c_mask)

                    if it >= LAG:
                        ci = it - LAG
                        pp = ci % 4

                        def c_av(tensor, j=j, hp=hp, ci=ci, pp=pp, gA=gA, gB=gB, Sj=Sj):
                            S.wait(tensor, tk_exp[ci])
                            if ci in tk_mask:
                                S.wait(tensor, tk_mask[ci])
                            if ci == 0 and prev_odone[0] is not None:
                                S.wait(tensor, prev_odone[0])
                            tensor.matmul(O_psA[:, :], V_sb[:, ci, gA * 65:(gA + 1) * 65],
                                          P_sb[pp][:, 0:512], start=(ci == 0), stop=(ci == Sj - 1))
                            la = tensor.matmul(O_psB[:, :], V_sb[:, ci, gB * 65:(gB + 1) * 65],
                                               P_sb[pp][:, 512:1024], start=(ci == 0), stop=(ci == Sj - 1))
                            tk_av[ci] = S.inc("mm", la)
                        blk.tensor(c_av)

                t_avl = tk_av[Sj - 1]

                def c_oev(vector, j=j, hp=hp, t_avl=t_avl):
                    S.wait(vector, t_avl)
                    if prev_osp[0] is not None:
                        S.wait(vector, prev_osp[0])
                    S.inc("ve", vector.tensor_copy(o_st[hp % 2][:, 0:512], O_psA[:, :]))
                    prev_odone[0] = S.inc("ve", vector.tensor_copy(
                        o_st[hp % 2][:, 512:1024], O_psB[:, :]))
                blk.vector(c_oev)
                tv_od = S.tok("ve")

                def c_osp(gpsimd, j=j, hp=hp, tv_od=tv_od, kt=kt):
                    S.wait(gpsimd, tv_od)
                    hA = kt * 8 + (hp % 4)
                    hB = hA + 4
                    S.inc("sp", gpsimd.dma_start(
                        out=o_spill[hA // 2, (hA % 2) * 64:(hA % 2) * 64 + 64, j * 512:(j + 1) * 512],
                        in_=o_st[hp % 2][0:64, 0:512]), 16)
                    S.inc("sp", gpsimd.dma_start(
                        out=o_spill[hB // 2, (hB % 2) * 64:(hB % 2) * 64 + 64, j * 512:(j + 1) * 512],
                        in_=o_st[hp % 2][0:64, 512:1024]), 16)
                    S.inc("sp", gpsimd.dma_start(out=z_spill[hA, j * 512:(j + 1) * 512],
                                                 in_=o_st[hp % 2][64:65, 0:512]), 16)
                    prev_osp[0] = S.inc("sp", gpsimd.dma_start(
                        out=z_spill[hB, j * 512:(j + 1) * 512],
                        in_=o_st[hp % 2][64:65, 512:1024]), 16)
                blk.gpsimd(c_osp)

            tk_msk_use[j] = S.tok("ve")
            tk_qt_use[j] = S.tok("mm")

        tC_bar = barrier_toks()
        cctx.close()
        attn_ctx.close()

        dctx = ExitStack()
        y_acc = SB(dctx, "y_acc", [128, 8 * 2048], F32)
        xn_h = SB(dctx, "xn_h", [128, 16 * 1024], BF16)
        xq_st = [SB(dctx, "xq_st0", [128, D], F32)]
        cb_row = SB(dctx, "cb_row", [128, 1024], BF16)
        cbc_sb = SB(dctx, "cbc_sb", [128, 1024], BF16)
        gsil = SB(dctx, "gsil", [128, 1024], BF16)
        hmul = SB(dctx, "hmul", [128, 1024], BF16)

        G_ps = [PS(dctx, f"G_ps{i}", [128, 512], F32) for i in range(2)]
        U_ps = [PS(dctx, f"U_ps{i}", [128, 512], F32) for i in range(2)]
        y_ps = [PS(dctx, f"y_ps{i}", [128, 512], F32) for i in range(2)]
        cb_ps2 = PS(dctx, "cb_ps2", [128, 1024], F32)

        gu_bf = [SB(dctx, f"gu_bf{i}", [128, 2048], BF16) for i in range(4)]
        gu_tok = {}
        gu_used = {}
        z_sb = SB(dctx, "z_sb", [32, 2048], BF16)
        rz_sb = z_sb
        zrow_st0 = SB(dctx, "zrow_st0", [128, 1024], BF16)
        zrow_st = [zrow_st0, zrow_st0]
        zbc_sb = SB(dctx, "zbc_sb", [128, 1024], BF16)
        t_rz = [None]
        prev_zbc = [None]
        y_store = [[None] * 8, [None] * 8]
        t_xnh_used = [None]
        prev_sil = [None]
        prev_hm = [None]
        prev_h = [None]
        prev_cbc_user = [None]
        last_ya = [None, None]
        prev_dn_last = [None]
        prev_yi_last = [None]
        tE_bar = None
        tW_bar = tC_bar

        for half in range(2):
            def d_xn(scalar, half=half):
                wait_bar(scalar, tC_bar)
                S.wait(scalar, t_xnsp)
                if half == 1:
                    S.wait(scalar, t_xnh_used[0])
                for k in range(16):
                    S.inc("la", scalar.dma_start(
                        out=xn_h[:, k * 1024:(k + 1) * 1024],
                        in_=xn_spill[:, k * 2048 + half * 1024:k * 2048 + (half + 1) * 1024]), 16)
            blk.scalar(d_xn)
            t_xnh = S.tok("la")

            tk_xq = [None] * 8
            tk_yi = [None] * 8
            for ts in range(8):
                def d_xq(sync, half=half, ts=ts):
                    wait_bar(sync, tC_bar)
                    if ts >= 1:
                        S.wait(sync, tk_yi[ts - 1])
                    elif half == 1:
                        S.wait(sync, prev_yi_last[0])
                    tk_xq[ts] = S.inc("ld", sync.dma_start(
                        out=xq_st[0][:, :],
                        in_=xq[half * 1024 + ts * 128:half * 1024 + (ts + 1) * 128, :]), 16)
                blk.sync(d_xq)

                def d_yi(vector, half=half, ts=ts):
                    S.wait(vector, tk_xq[ts])
                    if half == 1:
                        S.wait(vector, y_store[0][ts])
                    tk_yi[ts] = S.inc("ve", vector.tensor_copy(y_acc[:, ts * 2048:(ts + 1) * 2048],
                                                               xq_st[0][:, :]))
                blk.vector(d_yi)
            prev_yi_last[0] = tk_yi[7]

            ectx = ExitStack()
            dn_bf = [SB(ectx, f"dn_bf{half}_{i}", [128, 6 * 2048], BF16) for i in range(2)]
            H_sb = SB(ectx, f"H_sb{half}", [128, 6 * 1024], BF16)

            dn_tok = [None, None]
            dn_used = [None, None]
            t_cbrow = [None] * E
            t_cbc = [None] * E
            tk_h = {}

            def d_bar(gpsimd, half=half):
                wait_bar(gpsimd, tW_bar)
            blk.gpsimd(d_bar)

            def emit_gu_load(gt):
                if gt >= 2 * 12 * E:
                    return
                h2, r3 = divmod(gt, 12 * E)
                e2, r2 = divmod(r3, 12)
                fs2, w2 = divmod(r2, 2)
                srcw = gate_r if w2 == 0 else up_r

                def d_gul(gpsimd, gt=gt, e2=e2, fs2=fs2, srcw=srcw):
                    if gt >= 4:
                        S.wait(gpsimd, gu_used[gt - 4])
                    gu_tok[gt] = S.inc("lw", gpsimd.dma_start(
                        out=gu_bf[gt % 4][:, :].rearrange("p (k f) -> p k f", k=16),
                        in_=srcw[e2, fs2, :, :, :]), 16)
                blk.gpsimd(d_gul)

            if half == 0:
                for gt0 in range(4):
                    emit_gu_load(gt0)

            for e in range(E):
                eb = e % 2

                def d_we(gpsimd, half=half, e=e, eb=eb):
                    if e >= 2:
                        S.wait(gpsimd, dn_used[eb])
                    for fs in range(6):
                        dn_tok[eb] = S.inc("lw", gpsimd.dma_start(
                            out=dn_bf[eb][:, fs * 2048:(fs + 1) * 2048],
                            in_=down_r[e, fs * 128:(fs + 1) * 128, :]), 16)
                blk.gpsimd(d_we)

                def d_cb1(gpsimd, half=half, e=e):
                    if prev_cbc_user[0] is not None:
                        S.wait(gpsimd, prev_cbc_user[0])
                    t_cbrow[e] = S.inc("sp", gpsimd.dma_start(
                        out=cb_row[0:1, :], in_=comb_sb[e:e + 1, half * 1024:(half + 1) * 1024]), 16)
                blk.gpsimd(d_cb1)

                def d_cb2(tensor, e=e):
                    wait_bar(tensor, tW_bar)
                    S.wait(tensor, t_cbrow[e])
                    tensor.matmul(cb_ps2[:, 0:512], ones_b[0:1, 0:128], cb_row[0:1, 0:512],
                                  start=True, stop=True)
                    la = tensor.matmul(cb_ps2[:, 512:1024], ones_b[0:1, 0:128], cb_row[0:1, 512:1024],
                                       start=True, stop=True)
                    S.inc("mm", la)
                blk.tensor(d_cb2)
                tm_cb2 = S.tok("mm")

                def d_cb3(vector, e=e, tm_cb2=tm_cb2):
                    S.wait(vector, tm_cb2)
                    if prev_cbc_user[0] is not None:
                        S.wait(vector, prev_cbc_user[0])
                    t_cbc[e] = S.inc("ve", vector.tensor_copy(cbc_sb[:, :], cb_ps2[:, :]))
                blk.vector(d_cb3)

                tk_gmm = {}
                tk_umm = {}
                tk_sil = {}
                for fs in range(6):
                    gti = half * 12 * E + e * 12 + fs * 2
                    uti = gti + 1

                    def d_g(tensor, half=half, e=e, fs=fs, gti=gti):
                        S.wait(tensor, gu_tok[gti], t_xnh)
                        if fs >= 1:
                            S.wait(tensor, tk_sil[fs - 1])
                        elif prev_sil[0] is not None:
                            S.wait(tensor, prev_sil[0])
                        last = None
                        for k in range(16):
                            lhs = gu_bf[gti % 4][:, k * 128:(k + 1) * 128]
                            tensor.matmul(G_ps[0][:, :], lhs, xn_h[:, k * 1024:k * 1024 + 512],
                                          start=(k == 0), stop=(k == 15))
                            last = tensor.matmul(G_ps[1][:, :], lhs,
                                                 xn_h[:, k * 1024 + 512:(k + 1) * 1024],
                                                 start=(k == 0), stop=(k == 15))
                        tk_gmm[fs] = S.inc("mm", last)
                        gu_used[gti] = tk_gmm[fs]
                    blk.tensor(d_g)
                    emit_gu_load(gti + 4)

                    def d_u(tensor, half=half, e=e, fs=fs, uti=uti):
                        S.wait(tensor, gu_tok[uti])
                        if fs >= 1:
                            S.wait(tensor, tk_h.get((fs - 1, 1)))
                        elif prev_hm[0] is not None:
                            S.wait(tensor, prev_hm[0])
                        last = None
                        for k in range(16):
                            lhs = gu_bf[uti % 4][:, k * 128:(k + 1) * 128]
                            tensor.matmul(U_ps[0][:, :], lhs, xn_h[:, k * 1024:k * 1024 + 512],
                                          start=(k == 0), stop=(k == 15))
                            last = tensor.matmul(U_ps[1][:, :], lhs,
                                                 xn_h[:, k * 1024 + 512:(k + 1) * 1024],
                                                 start=(k == 0), stop=(k == 15))
                        tk_umm[fs] = S.inc("mm", last)
                        gu_used[uti] = tk_umm[fs]
                        if fs == 5:
                            t_xnh_used[0] = tk_umm[fs]
                    blk.tensor(d_u)
                    emit_gu_load(uti + 4)

                    def d_sil(scalar, e=e, fs=fs):
                        S.wait(scalar, tk_gmm[fs])
                        if prev_h[0] is not None:
                            S.wait(scalar, prev_h[0])
                        with nc.allow_low_precision("bf16 h-path by design"):
                            S.inc("sc", scalar.activation(gsil[:, 0:512], G_ps[0][:, :], AF.Silu))
                            tk_sil[fs] = S.inc("sc", scalar.activation(gsil[:, 512:1024],
                                                                       G_ps[1][:, :], AF.Silu))
                        prev_sil[0] = tk_sil[fs]
                    blk.scalar(d_sil)
                    ts_sil = S.tok("sc")

                    for tt in range(2):
                        def d_hm(vector, e=e, fs=fs, tt=tt, ts_sil=ts_sil):
                            S.wait(vector, ts_sil, tk_umm[fs])
                            if prev_h[0] is not None:
                                S.wait(vector, prev_h[0])
                            with nc.allow_low_precision("bf16 h-path by design"):
                                tk = S.inc("ve", vector.tensor_tensor(
                                    hmul[:, tt * 512:(tt + 1) * 512],
                                    gsil[:, tt * 512:(tt + 1) * 512],
                                    U_ps[tt][:, :], ALU.mult))
                            if tt == 1:
                                prev_hm[0] = tk
                            tk_h[(fs, tt, "hm")] = tk
                        blk.vector(d_hm)
                        tv_hm = S.tok("ve")

                        def d_h(gpsimd, e=e, fs=fs, tt=tt, tv_hm=tv_hm):
                            S.wait(gpsimd, tv_hm, t_cbc[e])
                            if fs == 0 and tt == 0 and prev_dn_last[0] is not None:
                                S.wait(gpsimd, prev_dn_last[0])
                            tk = S.inc("gp", gpsimd.tensor_tensor(
                                H_sb[:, fs * 1024 + tt * 512:fs * 1024 + (tt + 1) * 512],
                                hmul[:, tt * 512:(tt + 1) * 512],
                                cbc_sb[:, tt * 512:(tt + 1) * 512], ALU.mult))
                            tk_h[(fs, tt)] = tk
                            prev_h[0] = tk
                        blk.gpsimd(d_h)
                prev_cbc_user[0] = tk_h[(5, 1)]

                th_last = tk_h[(5, 1)]
                for ts in range(8):
                    for dt in range(4):
                        par = (ts * 4 + dt) % 2

                        def d_dn(tensor, e=e, eb=eb, ts=ts, dt=dt, par=par,
                                 th0=tk_h[(5, 0)], th1=tk_h[(5, 1)]):
                            S.wait(tensor, dn_tok[eb], th0 if ts < 4 else th1)
                            if last_ya[par] is not None:
                                S.wait(tensor, last_ya[par])
                            last = None
                            for fs in range(6):
                                last = tensor.matmul(
                                    y_ps[par][:, :],
                                    H_sb[:, fs * 1024 + ts * 128:fs * 1024 + (ts + 1) * 128],
                                    dn_bf[eb][:, fs * 2048 + dt * 512:fs * 2048 + (dt + 1) * 512],
                                    start=(fs == 0), stop=(fs == 5))
                            S.inc("mm", last)
                        blk.tensor(d_dn)
                        tm_dn = S.tok("mm")

                        def d_ya(vector, ts=ts, dt=dt, par=par, tm_dn=tm_dn):
                            S.wait(vector, tm_dn, tk_yi[ts])
                            sl = y_acc[:, ts * 2048 + dt * 512:ts * 2048 + (dt + 1) * 512]
                            last_ya[par] = S.inc("ve", vector.tensor_tensor(
                                sl, sl, y_ps[par][:, :], ALU.add))
                        blk.vector(d_ya)
                dn_used[eb] = S.tok("mm")
                prev_dn_last[0] = S.tok("mm")

            tE_bar = barrier_toks()
            ectx.close()

            wctx = ExitStack()
            O_h = SB(wctx, f"O_h{half}", [128, 16 * 1024], BF16)
            wo_bf = SB(wctx, f"wo_bf{half}", [128, 16 * 512], BF16)

            def d_oh(scalar, half=half):
                S.wait(scalar, prev_osp[0])
                wait_bar(scalar, tE_bar)
                if half == 0:
                    S.inc("la", scalar.dma_start(out=z_sb[:, :], in_=z_spill[:, :]), 16)
                for hp in range(16):
                    S.inc("la", scalar.dma_start(
                        out=O_h[:, hp * 1024:(hp + 1) * 1024],
                        in_=o_spill[hp, :, half * 1024:(half + 1) * 1024]), 16)
            blk.scalar(d_oh)
            t_oh = S.tok("la")

            if half == 0:
                def d_rz(vector):
                    S.wait(vector, t_oh)
                    with nc.allow_low_precision("1/Z kept in bf16 by design"):
                        t_rz[0] = S.inc("ve", vector.reciprocal(rz_sb[:, :], z_sb[:, :]))
                blk.vector(d_rz)

            tk_onorm = [None] * 16
            for hp in range(16):
                def d_zst(gpsimd, half=half, hp=hp):
                    S.wait(gpsimd, t_rz[0])
                    if prev_zbc[0] is not None:
                        S.wait(gpsimd, prev_zbc[0])
                    S.inc("sp", gpsimd.dma_start(
                        out=zrow_st[hp % 2][0:1, :],
                        in_=rz_sb[2 * hp:2 * hp + 1, half * 1024:(half + 1) * 1024]), 16)
                    S.inc("sp", gpsimd.dma_start(
                        out=zrow_st[hp % 2][64:65, :],
                        in_=rz_sb[2 * hp + 1:2 * hp + 2, half * 1024:(half + 1) * 1024]), 16)
                blk.gpsimd(d_zst)
                t_zst = S.tok("sp")

                def d_zbc(tensor, half=half, hp=hp, t_zst=t_zst):
                    S.wait(tensor, t_zst)
                    if hp >= 1:
                        S.wait(tensor, tk_onorm[hp - 1])
                    tensor.matmul(cb_ps2[0:64, 0:512], ones_b[0:1, 0:64],
                                  zrow_st[hp % 2][0:1, 0:512], start=True, stop=True)
                    tensor.matmul(cb_ps2[0:64, 512:1024], ones_b[0:1, 0:64],
                                  zrow_st[hp % 2][0:1, 512:1024], start=True, stop=True)
                    tensor.matmul(cb_ps2[64:128, 0:512], ones_b[64:65, 0:64],
                                  zrow_st[hp % 2][64:65, 0:512], start=True, stop=True)
                    la = tensor.matmul(cb_ps2[64:128, 512:1024], ones_b[64:65, 0:64],
                                       zrow_st[hp % 2][64:65, 512:1024], start=True, stop=True)
                    prev_zbc[0] = S.inc("mm", la)
                blk.tensor(d_zbc)
                tm_zbc = S.tok("mm")

                def d_onorm(vector, half=half, hp=hp, tm_zbc=tm_zbc):
                    S.wait(vector, tm_zbc, t_oh)
                    S.inc("ve", vector.tensor_copy(zbc_sb[:, :], cb_ps2[:, :]))
                    S.wait(vector, S.tok("ve"))
                    sl = O_h[:, hp * 1024:(hp + 1) * 1024]
                    tk_onorm[hp] = S.inc("ve", vector.tensor_tensor(sl, sl, zbc_sb[:, :], ALU.mult))
                blk.vector(d_onorm)
            t_onorm_all = tk_onorm[15]

            tk_wo = [None] * 4
            tk_womm = {}
            tk_woya = {}
            for dt in range(4):
                def d_wol(gpsimd, half=half, dt=dt):
                    wait_bar(gpsimd, tE_bar)
                    if dt >= 1:
                        S.wait(gpsimd, tk_womm[(dt - 1, 7)])
                    tk_wo[dt] = S.inc("lw", gpsimd.dma_start(
                        out=wo_bf[:, :].rearrange("p (k f) -> p k f", k=16),
                        in_=wo_r[dt, :, :, :]), 16)
                blk.gpsimd(d_wol)

                for ts in range(8):
                    par = (dt * 8 + ts) % 2

                    def d_wom(tensor, half=half, dt=dt, ts=ts, par=par):
                        wait_bar(tensor, tE_bar)
                        S.wait(tensor, tk_wo[dt], t_oh, t_onorm_all)
                        if last_ya[par] is not None:
                            S.wait(tensor, last_ya[par])
                        last = None
                        for k in range(16):
                            last = tensor.matmul(
                                y_ps[par][:, :],
                                O_h[:, k * 1024 + ts * 128:k * 1024 + (ts + 1) * 128],
                                wo_bf[:, k * 512:(k + 1) * 512],
                                start=(k == 0), stop=(k == 15))
                        tk_womm[(dt, ts)] = S.inc("mm", last)
                    blk.tensor(d_wom)

                    def d_woya(vector, dt=dt, ts=ts, par=par):
                        S.wait(vector, tk_womm[(dt, ts)])
                        sl = y_acc[:, ts * 2048 + dt * 512:ts * 2048 + (dt + 1) * 512]
                        tk_woya[(dt, ts)] = S.inc("ve", vector.tensor_tensor(
                            sl, sl, y_ps[par][:, :], ALU.add))
                        last_ya[par] = tk_woya[(dt, ts)]
                    blk.vector(d_woya)

            for ts in range(8):
                def d_st(scalar, half=half, ts=ts):
                    for dt in range(4):
                        S.wait(scalar, tk_woya[(dt, ts)])
                    y_store[half][ts] = S.inc("so", scalar.dma_start(
                        out=out[half * 1024 + ts * 128:half * 1024 + (ts + 1) * 128, :],
                        in_=y_acc[:, ts * 2048:(ts + 1) * 2048]), 16)
                blk.scalar(d_st)
            tW_bar = barrier_toks()
            wctx.close()

        if DEBUG:
            def dbg(sync):
                S.wait(sync, t_xnsp, prev_osp[0])
                S.wait(sync, *[tk_qsp[ct][NQT - 1] for ct in range(4)])
                S.inc("so", sync.dma_start(out=dbg_xn[:, :], in_=xn_spill[:, :]), 16)
                for hp in range(16):
                    S.inc("so", sync.dma_start(out=dbg_q[hp, :, :], in_=q_spill[hp, :, :]), 16)
                    S.inc("so", sync.dma_start(out=dbg_o[hp, :, :], in_=o_spill[hp, :, :]), 16)
                S.inc("so", sync.dma_start(out=dbg_cb[:, :], in_=comb_sb[:, :]), 16)
                S.inc("so", sync.dma_start(out=dbg_bias[:, 0:512], in_=bias_x[:, :]), 16)
                S.inc("so", sync.dma_start(out=dbg_bias[:, 512:576], in_=bias_p[:, :]), 16)
                with nc.allow_non_contiguous_dma("debug only"):
                    S.inc("so", sync.dma_start(out=dbg_bias[:, 576:577], in_=knm2[:, :]), 16)
            blk.sync(dbg)

        def fin(scalar):
            S.wait(scalar, S.tok("so"))
        blk.scalar(fin)
        dctx.close()

    with nc.Block() as _blk:
        _blk.sync(lambda e: _emit("sync", e))
        _blk.vector(lambda e: _emit("vector", e))
        _blk.scalar(lambda e: _emit("scalar", e))
        _blk.tensor(lambda e: _emit("tensor", e))
        _blk.gpsimd(lambda e: _emit("gpsimd", e))

    ctx.close()
    return nc


_CACHE = {}


def _host_prep(L0_attn_x, L0_attn_wq, L0_attn_wk, L0_attn_wv, L0_attn_wo,
               L0_ffn_router_w, L0_ffn_gate_w, L0_ffn_up_w, L0_ffn_down_w):
    f32 = np.float32
    wq = np.asarray(L0_attn_wq, f32)
    wk = np.asarray(L0_attn_wk, f32)
    wv = np.asarray(L0_attn_wv, f32)
    wo = np.asarray(L0_attn_wo, f32)
    rw = np.asarray(L0_ffn_router_w, f32)
    gw = np.asarray(L0_ffn_gate_w, f32)
    uw = np.asarray(L0_ffn_up_w, f32)
    dw = np.asarray(L0_ffn_down_w, f32)
    x = np.asarray(L0_attn_x, f32)

    wkT = np.ascontiguousarray(wk.T)
    wvT = np.ascontiguousarray(wv.T)
    wqT = np.ascontiguousarray(wq.T)
    woT = np.ascontiguousarray(wo.T)
    wk_r = np.ascontiguousarray(wkT.reshape(16, 128, 512).transpose(1, 0, 2))
    wv_r = np.ascontiguousarray(wvT.reshape(16, 128, 512).transpose(1, 0, 2))
    wq_r = np.ascontiguousarray(wqT.reshape(16, 128, 4, 512).transpose(2, 1, 0, 3))
    wo_r = np.ascontiguousarray(woT.reshape(16, 128, 4, 512).transpose(2, 1, 0, 3))
    rt_r = np.ascontiguousarray(rw.reshape(16, 128, 16).transpose(1, 0, 2))
    gate_r = np.ascontiguousarray(
        gw.reshape(E, 16, 128, 6, 128).transpose(0, 3, 2, 1, 4))
    up_r = np.ascontiguousarray(
        uw.reshape(E, 16, 128, 6, 128).transpose(0, 3, 2, 1, 4))
    down_r = np.ascontiguousarray(dw)

    i32 = np.arange(0, HD, 2, dtype=f32) / np.float32(HD)
    theta = (1.0 / (np.float32(10000.0) ** i32)).astype(f32)
    tpos = np.arange(T, dtype=f32)
    freqs = (tpos[:, None] * theta[None, :]).astype(f32)
    cosf = np.cos(freqs).astype(f32)
    sinf = np.sin(freqs).astype(f32)
    cosk = np.ascontiguousarray(np.tile(cosf, (1, HKV)))
    sink = np.ascontiguousarray(np.tile(sinf, (1, HKV)))

    identb = np.eye(128, dtype=BFd)
    identf = np.eye(128, dtype=f32)

    common = dict(wk_r=wk_r, wv_r=wv_r, wq_r=wq_r, wo_r=wo_r, rt_r=rt_r,
                  gate_r=gate_r, up_r=up_r, down_r=down_r,
                  cosk=cosk, sink=sink, identb=identb, identf=identf)

    in_maps = []
    pos_locals = []
    for c in range(8):
        seq, half = c // 2, c % 2
        starts = TILE_STARTS[half]
        pos_local = np.concatenate([np.arange(s0, s0 + 512) for s0 in starts])
        pos_locals.append(pos_local)
        xs = np.ascontiguousarray(x[seq])
        xqs = np.ascontiguousarray(x[seq][pos_local])
        cosq = np.ascontiguousarray(np.tile(cosf[pos_local], (1, H)))
        sinq = np.ascontiguousarray(np.tile(sinf[pos_local], (1, H)))
        ncc = [(s0 + 512) // 128 for s0 in starts]
        maskd = np.ones((4, 8, 128, 1024), dtype=BFd)
        for j in range(4):
            t0 = starts[j]
            for mi, cch in enumerate(DIAG_STRUCT[j]):
                if cch >= ncc[j]:
                    continue
                keys = cch * 128 + np.arange(128)[:, None]
                qs = t0 + np.arange(512)[None, :]
                m = (keys <= qs).astype(np.float32).astype(BFd)
                maskd[j, mi, :, 0:512] = m
                maskd[j, mi, :, 512:1024] = m
        excrow = np.zeros((4, 128), dtype=f32)
        for j in range(4):
            for ci, cch in enumerate(EXC_STRUCT[j]):
                if cch >= ncc[j]:
                    excrow[j, ci * 16:(ci + 1) * 16] = -1e30
        m = dict(common)
        m.update(x=xs, xq=xqs, cosq=cosq, sinq=sinq,
                 maskd=np.ascontiguousarray(maskd),
                 excrow=np.ascontiguousarray(excrow))
        in_maps.append(m)
    return in_maps, pos_locals


def kernel(L0_attn_x, L0_attn_wq, L0_attn_wk, L0_attn_wv, L0_attn_wo,
           L0_ffn_router_w, L0_ffn_gate_w, L0_ffn_up_w, L0_ffn_down_w):
    in_maps, pos_locals = _host_prep(
        L0_attn_x, L0_attn_wq, L0_attn_wk, L0_attn_wv, L0_attn_wo,
        L0_ffn_router_w, L0_ffn_gate_w, L0_ffn_up_w, L0_ffn_down_w)
    if "nc" not in _CACHE:
        _CACHE["nc"] = build_nc()
    nc = _CACHE["nc"]
    res = run_bass_kernel_spmd(nc, in_maps, core_ids=list(range(8)))
    _CACHE["res"] = res
    y = np.empty((B, T, D), dtype=np.float32)
    for c in range(8):
        seq = c // 2
        y[seq, pos_locals[c], :] = np.asarray(res.results[c]["out"], dtype=np.float32)
    return y


# version 12
# speedup vs baseline: 1.2515x; 1.2515x over previous
import numpy as np
import ml_dtypes
import concourse.bass as bass
import concourse.mybir as mybir
from concourse.bass_utils import run_bass_kernel_spmd
from contextlib import ExitStack

F32 = mybir.dt.float32
BF16 = mybir.dt.bfloat16
AF = mybir.ActivationFunctionType
ALU = mybir.AluOpType
AX = mybir.AxisListType
BFd = ml_dtypes.bfloat16

B, T, D = 4, 4096, 2048
H, HKV, HD = 32, 8, 64
E, FF = 16, 768
NSL = T // 128
NQT = 16
S_STRUCT = [32, 28, 16, 12]
EXC_STRUCT = [list(range(24, 32)), list(range(20, 28)),
              list(range(8, 16)), list(range(4, 12))]
DIAG_STRUCT = [sorted(set(range(28, 32)) | set(range(20, 24))),
               sorted(set(range(24, 28)) | set(range(16, 20))),
               sorted(set(range(4, 8)) | set(range(12, 16))),
               sorted(set(range(0, 4)) | set(range(8, 12)))]
TILE_STARTS = {0: [3584, 3072, 512, 0], 1: [2560, 2048, 1536, 1024]}
DEBUG = False


class _MI:
    def then_inc(self, *a, **k):
        return self


class _ME:

    def __init__(self, name):
        self.engine = "mock." + name

    def __getattr__(self, item):
        return lambda *a, **k: _MI()


class Sch:

    def __init__(self):
        self.cnt, self.h, self.lastw = {}, {}, {}

    def reset(self):
        for k in self.cnt:
            self.cnt[k] = 0
        self.lastw.clear()

    def add(self, name, handle):
        self.h[name] = handle
        self.cnt[name] = 0

    def inc(self, name, instr, n=1):
        instr.then_inc(self.h[name], n)
        self.cnt[name] += n
        return (name, self.cnt[name])

    def tok(self, name):
        return (name, self.cnt[name])

    def wait(self, eng, *toks):
        key = str(eng.engine)
        for t in toks:
            if t is None:
                continue
            nm, v = t
            if v <= 0:
                continue
            if self.lastw.get((key, nm), 0) < v:
                eng.wait_ge(self.h[nm], v)
                self.lastw[(key, nm)] = v


def build_nc():
    nc = bass.Bass("TRN2", target_bir_lowering=False)

    x = nc.dram_tensor("x", [T, D], F32, kind="ExternalInput")
    xq = nc.dram_tensor("xq", [2048, D], F32, kind="ExternalInput")
    wk_r = nc.dram_tensor("wk_r", [128, 16, 512], F32, kind="ExternalInput")
    wv_r = nc.dram_tensor("wv_r", [128, 16, 512], F32, kind="ExternalInput")
    wq_r = nc.dram_tensor("wq_r", [4, 128, 16, 512], F32, kind="ExternalInput")
    wo_r = nc.dram_tensor("wo_r", [4, 128, 16, 512], F32, kind="ExternalInput")
    rt_r = nc.dram_tensor("rt_r", [128, 16, 16], F32, kind="ExternalInput")
    gate_r = nc.dram_tensor("gate_r", [E, 6, 128, 16, 128], F32, kind="ExternalInput")
    up_r = nc.dram_tensor("up_r", [E, 6, 128, 16, 128], F32, kind="ExternalInput")
    down_r = nc.dram_tensor("down_r", [E, 768, 2048], F32, kind="ExternalInput")
    cosk = nc.dram_tensor("cosk", [T, 256], F32, kind="ExternalInput")
    sink = nc.dram_tensor("sink", [T, 256], F32, kind="ExternalInput")
    cosq = nc.dram_tensor("cosq", [2048, 1024], F32, kind="ExternalInput")
    sinq = nc.dram_tensor("sinq", [2048, 1024], F32, kind="ExternalInput")
    identb = nc.dram_tensor("identb", [128, 128], BF16, kind="ExternalInput")
    identf = nc.dram_tensor("identf", [128, 128], F32, kind="ExternalInput")
    maskd = nc.dram_tensor("maskd", [4, 8, 128, 1024], BF16, kind="ExternalInput")
    excrow = nc.dram_tensor("excrow", [4, 128], F32, kind="ExternalInput")
    out = nc.dram_tensor("out", [2048, D], F32, kind="ExternalOutput")
    if DEBUG:
        dbg_xn = nc.dram_tensor("dbg_xn", [128, 16 * 2048], BF16, kind="ExternalOutput")
        dbg_q = nc.dram_tensor("dbg_q", [16, 128, 2048], BF16, kind="ExternalOutput")
        dbg_o = nc.dram_tensor("dbg_o", [16, 128, 2048], BF16, kind="ExternalOutput")
        dbg_cb = nc.dram_tensor("dbg_cb", [16, 2048], BF16, kind="ExternalOutput")
        dbg_bias = nc.dram_tensor("dbg_bias", [128, 4 * 128 + 4 * 16 + 1], F32, kind="ExternalOutput")

    q_spill = nc.dram_tensor("q_spill", [16, 128, 2048], BF16, kind="Internal")
    z_rec = nc.dram_tensor("z_rec", [32, 2048], BF16, kind="Internal")
    o_spill = nc.dram_tensor("o_spill", [16, 128, 2048], BF16, kind="Internal")
    xn_spill = nc.dram_tensor("xn_spill", [128, 16 * 2048], BF16, kind="Internal")
    z_spill = nc.dram_tensor("z_spill", [32, 2048], BF16, kind="Internal")

    S = Sch()
    ctx = ExitStack()
    for nm in ["ld", "lw", "la", "sp", "mm", "ve", "sc", "gp", "so"]:
        S.add(nm, ctx.enter_context(nc.semaphore(nm)))

    _alloc_cache = {}

    def SB(st, n, sh, dt):
        if n not in _alloc_cache:
            _alloc_cache[n] = st.enter_context(nc.sbuf_tensor(n, sh, dt))
        return _alloc_cache[n]

    def PS(st, n, sh, dt):
        if n not in _alloc_cache:
            _alloc_cache[n] = st.enter_context(nc.psum_tensor(n, sh, dt))
        return _alloc_cache[n]

    idb = SB(ctx, "idb", [128, 128], BF16)
    idf = SB(ctx, "idf", [128, 128], F32)
    ones_b = SB(ctx, "ones_b", [128, 128], BF16)
    c_eps = SB(ctx, "c_eps", [128, 1], F32)
    c_s8 = SB(ctx, "c_s8", [128, 1], F32)
    c_nh = SB(ctx, "c_nh", [128, 1], F32)
    c_hl = SB(ctx, "c_hl", [128, 1], F32)
    c_lb = SB(ctx, "c_lb", [128, 1], F32)
    comb_tok = SB(ctx, "comb_tok", [128, 256], F32)
    knmax = SB(ctx, "knmax", [128, 8], F32)
    knm2 = SB(ctx, "knm2", [128, 1], F32)
    knsc_bf = SB(ctx, "knsc_bf", [128, 1], BF16)
    negp = SB(ctx, "negp", [16, 4], F32)
    negrow = SB(ctx, "negrow", [128, 4 * 16], F32)
    bias_p = SB(ctx, "bias_p", [128, 4 * 16], F32)
    bias_x = SB(ctx, "bias_x", [128, 4 * 128], F32)
    qn2 = SB(ctx, "qn2", [128, 16 * 32], F32)
    qn2p = SB(ctx, "qn2p", [128, 16 * 16], F32)
    attn_ctx = ExitStack()
    K_cmp = SB(attn_ctx, "K_cmp", [128, 4, T], BF16)
    V_sb = SB(attn_ctx, "V_sb", [128, NSL, 520], BF16)

    def barrier_toks():
        return [S.tok(n) for n in ["ld", "lw", "la", "sp", "mm", "ve", "sc", "gp"]]

    _mocks = {n: _ME(n) for n in ["sync", "vector", "scalar", "tensor", "gpsimd"]}

    def _emit(cur, real_eng):
        S.reset()

        class _B:
            def __getattr__(self, name):
                def run(f):
                    f(real_eng if name == cur else _mocks[name])
                return run
        blk = _B()
        def c0(vector):
            vector.memset(c_eps[:, :], 1e-5)
            vector.memset(c_s8[:, :], 0.125)
            vector.memset(c_nh[:, :], -0.5)
            vector.memset(c_hl[:, :], 0.5)
            vector.memset(c_lb[:, :], float(np.log(1.01 / 8.0)))
            vector.memset(ones_b[:, :], 1.0)
            vector.memset(knmax[:, :], 0.0)
            vg = V_sb[:, :, :].rearrange("p c (g y) -> p (c g) y", y=65)
            S.inc("ve", vector.memset(vg[:, :, 64:65], 1.0))
        blk.vector(c0)
        t_consts = S.tok("ve")

        def c1(sync):
            S.inc("ld", sync.dma_start(out=idb[:, :], in_=identb[:, :]), 16)
            S.inc("ld", sync.dma_start(out=idf[:, :], in_=identf[:, :]), 16)
        blk.sync(c1)
        t_ident = S.tok("ld")

        actx = ExitStack()
        wk_bf = SB(actx, "wk_bf", [128, 16 * 512], BF16)
        wv_bf = SB(actx, "wv_bf", [128, 16 * 512], BF16)
        x_sl = [SB(actx, f"x_sl{i}", [128, D], F32) for i in range(4)]
        xn_sl = [SB(actx, f"xn_sl{i}", [128, D], BF16) for i in range(4)]
        xnf_sl = [SB(actx, f"xnf_sl{i}", [128, 16 * 128], BF16) for i in range(4)]
        k_tm = [SB(actx, f"k_tm{i}", [128, 512], F32) for i in range(4)]
        krot = [SB(actx, f"krot{i}", [128, 512], BF16) for i in range(4)]
        rtab = [SB(actx, f"rtab{i}", [128, 512], F32) for i in range(4)]
        sqa = SB(actx, "sqa", [128, 512], F32)
        st_ln_r = [SB(actx, f"st_ln{i}", [128, 32], F32) for i in range(4)]
        kn2 = SB(actx, "kn2", [128, 8], F32)
        rtv = [SB(actx, f"rtv{i}", [128, 256], F32) for i in range(2)]
        rtg = [SB(actx, f"rtg{i}", [128, 256], F32) for i in range(2)]

        tr_ps = [PS(actx, f"tr_ps{i}", [128, 1024], BF16) for i in range(2)]
        K_ps = [PS(actx, f"K_ps{i}", [128, 512], F32) for i in range(2)]
        V_ps = [PS(actx, f"V_ps{i}", [128, 512], F32) for i in range(2)]
        ktr_ps = [PS(actx, f"ktr_ps{i}", [128, 512], BF16) for i in range(2)]

        def a_w(gpsimd):
            S.inc("lw", gpsimd.dma_start(out=wk_bf[:, :].rearrange("p (k f) -> p k f", k=16),
                                         in_=wk_r[:, :, :]), 16)
            S.inc("lw", gpsimd.dma_start(out=wv_bf[:, :].rearrange("p (k f) -> p k f", k=16),
                                         in_=wv_r[:, :, :]), 16)
        blk.gpsimd(a_w)
        t_wkv = S.tok("lw")

        tk_xload = [None] * NSL
        tk_xn = [None] * NSL
        tk_xnf = [None] * NSL
        tk_kv = [None] * NSL
        tk_kevac = [None] * NSL
        tk_krot = [None] * NSL
        tk_rtab = [None] * NSL
        tk_ktr = [None] * NSL
        tk_kcmp = [None] * NSL
        tk_vevac = [None] * NSL
        tk_kn = [None] * NSL
        tk_knsq = [None] * NSL
        tk_ropev = [None] * NSL
        tk_sum = [None] * NSL
        tk_rstd = [None] * NSL
        tk_tr = [None] * NSL

        def st_ln(s):
            b = s % 4

            def a_load(sync, s=s, b=b):
                if s >= 4:
                    S.wait(sync, tk_xn[s - 4])
                tk_xload[s] = S.inc("ld", sync.dma_start(
                    out=x_sl[b][:, :], in_=x[s * 128:(s + 1) * 128, :]), 16)
                if s >= 4:
                    S.wait(sync, tk_krot[s - 4], tk_ropev[s - 4])
                S.inc("ld", sync.dma_start(out=rtab[b][:, 0:256],
                                           in_=cosk[s * 128:(s + 1) * 128, :]), 16)
                tk_rtab[s] = S.inc("ld", sync.dma_start(
                    out=rtab[b][:, 256:512], in_=sink[s * 128:(s + 1) * 128, :]), 16)
            blk.sync(a_load)

            def a_bn(vector, s=s, b=b):
                S.wait(vector, tk_xload[s])
                if s >= 4:
                    S.wait(vector, tk_xn[s - 4])
                st_lnx = st_ln_r[b]
                for q4 in range(4):
                    S.inc("ve", vector.bn_stats(st_lnx[:, 8 + q4 * 6:8 + (q4 + 1) * 6],
                                                x_sl[b][:, q4 * 512:(q4 + 1) * 512]))
                S.wait(vector, S.tok("ve"))
                S.inc("ve", vector.bn_aggr(st_lnx[:, 2:4], st_lnx[:, 8:32]))
            blk.vector(a_bn)
            tv_var = S.tok("ve")

            def a_rstd(scalar, s=s, b=b):
                S.wait(scalar, tv_var, t_consts)
                st_lnx = st_ln_r[b]
                S.inc("sc", scalar.activation(st_lnx[:, 3:4], st_lnx[:, 3:4], AF.Ln, bias=c_eps[:, 0:1]))
                S.wait(scalar, S.tok("sc"))
                tk_rstd[s] = S.inc("sc", scalar.activation(st_lnx[:, 3:4], st_lnx[:, 3:4],
                                                           AF.Exp, scale=c_nh[:, 0:1]))
            blk.scalar(a_rstd)

        def st_x(s):
            b = s % 4

            def a_xn(vector, s=s, b=b):
                S.wait(vector, tk_rstd[s])
                if s >= 4:
                    S.wait(vector, tk_xnf[s - 4])
                st_lnx = st_ln_r[b]
                tk_xn[s] = S.inc("ve", vector.tensor_scalar(
                    xn_sl[b][:, :], x_sl[b][:, :], st_lnx[:, 2:3], st_lnx[:, 3:4],
                    op0=ALU.subtract, op1=ALU.mult))
            blk.vector(a_xn)

            def a_tr(tensor, s=s, b=b):
                S.wait(tensor, tk_xn[s], t_ident)
                if s >= 1:
                    S.wait(tensor, tk_xnf[s - 1])
                for g in range(2):
                    last = None
                    for i in range(8):
                        k = g * 8 + i
                        last = tensor.transpose(tr_ps[g][:, i * 128:(i + 1) * 128],
                                                xn_sl[b][:, k * 128:(k + 1) * 128], idb[:, :])
                    S.inc("mm", last)
                tk_tr[s] = S.tok("mm")
            blk.tensor(a_tr)

        def st_kv(s):
            b = s % 4

            def a_trev(vector, s=s, b=b):
                if s >= 4:
                    S.wait(vector, tk_kv[s - 4])
                S.wait(vector, ("mm", tk_tr[s][1] - 1))
                S.inc("ve", vector.tensor_copy(xnf_sl[b][:, 0:1024], tr_ps[0][:, :]))
                S.wait(vector, tk_tr[s])
                tk_xnf[s] = S.inc("ve", vector.tensor_copy(xnf_sl[b][:, 1024:2048], tr_ps[1][:, :]))
            blk.vector(a_trev)

            def a_kv(tensor, s=s, b=b):
                S.wait(tensor, tk_xnf[s], t_wkv)
                if s >= 2:
                    S.wait(tensor, tk_kevac[s - 2], tk_vevac[s - 2])
                last = None
                for k in range(16):
                    lhs = xnf_sl[b][:, k * 128:(k + 1) * 128]
                    tensor.matmul(K_ps[s % 2][:, :], lhs, wk_bf[:, k * 512:(k + 1) * 512],
                                  start=(k == 0), stop=(k == 15))
                    last = tensor.matmul(V_ps[s % 2][:, :], lhs, wv_bf[:, k * 512:(k + 1) * 512],
                                         start=(k == 0), stop=(k == 15))
                tk_kv[s] = S.inc("mm", last)
            blk.tensor(a_kv)

        def st_k1(s):
            b = s % 4

            def a_kev(vector, s=s, b=b):
                S.wait(vector, tk_kv[s])
                if s >= 4:
                    S.wait(vector, tk_krot[s - 4], tk_ropev[s - 4])
                tk_kevac[s] = S.inc("ve", vector.tensor_copy(k_tm[b][:, :], K_ps[s % 2][:, :]))
            blk.vector(a_kev)

            def a_vev(vector, s=s, b=b):
                S.wait(vector, tk_kv[s], t_consts)
                vg = V_sb[:, s, :].rearrange("p (g y) -> p g y", y=65)
                tk_vevac[s] = S.inc("ve", vector.tensor_copy(
                    vg[:, :, 0:64], V_ps[s % 2][:, :].rearrange("p (g y) -> p g y", y=64)))
            blk.vector(a_vev)

            def a_rg(gpsimd, s=s, b=b):
                S.wait(gpsimd, tk_kevac[s], tk_rtab[s])
                if s >= 4:
                    S.wait(gpsimd, tk_ktr[s - 4], tk_knsq[s - 4])
                kt = k_tm[b][:, :].rearrange("p (h i two) -> p h i two", two=2, i=32)
                kr = krot[b][:, :].rearrange("p (h i two) -> p h i two", two=2, i=32)
                cosv = rtab[b][:, 0:256].rearrange("p (h i) -> p h i", i=32)
                sinv = rtab[b][:, 256:512].rearrange("p (h i) -> p h i", i=32)
                r0 = rtv[0][:, :].rearrange("p (h i) -> p h i", i=32)
                r1 = rtv[1][:, :].rearrange("p (h i) -> p h i", i=32)
                g0 = rtg[0][:, :].rearrange("p (h i) -> p h i", i=32)
                g1 = rtg[1][:, :].rearrange("p (h i) -> p h i", i=32)
                S.inc("gp", gpsimd.tensor_tensor(r0, kt[:, :, :, 1], sinv, ALU.mult))
                S.inc("gp", gpsimd.tensor_tensor(r1, kt[:, :, :, 0], cosv, ALU.mult))
                S.inc("gp", gpsimd.tensor_tensor(g0, kt[:, :, :, 0], sinv, ALU.mult))
                S.inc("gp", gpsimd.tensor_tensor(g1, kt[:, :, :, 1], cosv, ALU.mult))
                S.wait(gpsimd, S.tok("gp"))
                S.inc("gp", gpsimd.tensor_tensor(kr[:, :, :, 0], r1, r0, ALU.subtract))
                tk_krot[s] = S.inc("gp", gpsimd.tensor_tensor(kr[:, :, :, 1], g0, g1, ALU.add))
                tk_ropev[s] = tk_krot[s]
            blk.gpsimd(a_rg)

        def st_k2(s):
            b = s % 4

            def a_ktr(tensor, s=s, b=b):
                S.wait(tensor, tk_krot[s], tk_ropev[s])
                if s >= 2:
                    S.wait(tensor, tk_kcmp[s - 2])
                last = None
                for t4 in range(4):
                    last = tensor.transpose(ktr_ps[s % 2][:, t4 * 128:(t4 + 1) * 128],
                                            krot[b][:, t4 * 128:(t4 + 1) * 128], idb[:, :])
                tk_ktr[s] = S.inc("mm", last)
            blk.tensor(a_ktr)

            def a_kcm(vector, s=s, b=b):
                S.wait(vector, tk_ktr[s])
                tk_kcmp[s] = S.inc("ve", vector.tensor_copy(
                    K_cmp[:, :, s * 128:(s + 1) * 128],
                    ktr_ps[s % 2][:, :].rearrange("p (t c) -> p t c", t=4)))
            blk.vector(a_kcm)

            def a_knsq(gpsimd, s=s, b=b):
                S.wait(gpsimd, tk_krot[s])
                if s >= 1:
                    S.wait(gpsimd, tk_kn[s - 1])
                tk_knsq[s] = S.inc("gp", gpsimd.tensor_tensor(sqa[:, :], krot[b][:, :],
                                                              krot[b][:, :], ALU.mult))
            blk.gpsimd(a_knsq)

            def a_kn(vector, s=s):
                S.wait(vector, tk_knsq[s])
                S.inc("ve", vector.reduce_sum(kn2[:, :], sqa[:, :]
                                              .rearrange("p (g y) -> p g y", y=64), axis=AX.X))
                S.wait(vector, S.tok("ve"))
                tk_kn[s] = S.inc("ve", vector.tensor_max(knmax[:, :], knmax[:, :], kn2[:, :]))
            blk.vector(a_kn)

        for it_a in range(NSL + 4):
            if 4 <= it_a and it_a - 4 < NSL:
                st_k2(it_a - 4)
            if 3 <= it_a and it_a - 3 < NSL:
                st_k1(it_a - 3)
            if 2 <= it_a and it_a - 2 < NSL:
                st_kv(it_a - 2)
            if 1 <= it_a and it_a - 1 < NSL:
                st_x(it_a - 1)
            if it_a < NSL:
                st_ln(it_a)

        def a_kf1(vector):
            S.wait(vector, tk_kn[NSL - 1])
            S.inc("ve", vector.reduce_max(kn2[:, 0:1], knmax[:, :], axis=AX.X))
        blk.vector(a_kf1)
        tv1 = S.tok("ve")

        def a_kf2(tensor):
            S.wait(tensor, tv1)
            S.inc("mm", tensor.transpose(K_ps[0][0:1, 0:128], kn2[:, 0:1], idf[:, :]))
        blk.tensor(a_kf2)
        tm1 = S.tok("mm")

        def a_kf3(vector):
            S.wait(vector, tm1)
            S.inc("ve", vector.reduce_max(st_ln_r[0][0:1, 0:1], K_ps[0][0:1, 0:128], axis=AX.X))
            S.wait(vector, S.tok("ve"))
            S.inc("ve", vector.tensor_copy(knsc_bf[0:1, 0:1], st_ln_r[0][0:1, 0:1]))
        blk.vector(a_kf3)
        tv2 = S.tok("ve")

        def a_kf4(tensor):
            S.wait(tensor, tv2)
            S.inc("mm", tensor.matmul(K_ps[1][:, 0:1], ones_b[0:1, 0:128], knsc_bf[0:1, 0:1],
                                      start=True, stop=True))
        blk.tensor(a_kf4)
        tm2 = S.tok("mm")

        def a_kf5(vector):
            S.wait(vector, tm2)
            S.inc("ve", vector.tensor_copy(knm2[:, :], K_ps[1][:, 0:1]))
        blk.vector(a_kf5)

        tA_bar = barrier_toks()
        actx.close()

        bctx = ExitStack()
        xn_q = SB(bctx, "xn_q", [128, 16 * 2048], BF16)
        sqb = SB(bctx, "sqb", [128, 512], F32)

        def wait_bar(eng, bar):
            S.wait(eng, *bar)

        b1 = ExitStack()
        x_slB = [SB(b1, f"x_slB{i}", [128, D], F32) for i in range(4)]
        xn_slB = [SB(b1, f"xn_slB{i}", [128, D], BF16) for i in range(4)]
        st_lnB_r = [SB(b1, f"st_lnB{i}", [128, 32], F32) for i in range(4)]
        tr_psB = [PS(b1, f"tr_psB{i}", [128, 1024], BF16) for i in range(2)]
        tkB_x = [None] * NQT
        tkB_xn = [None] * NQT
        tkB_q = [None] * NQT
        for s in range(NQT):
            b = s % 4

            def b_load(sync, s=s, b=b):
                wait_bar(sync, tA_bar)
                if s >= 4:
                    S.wait(sync, tkB_xn[s - 4])
                tkB_x[s] = S.inc("ld", sync.dma_start(
                    out=x_slB[b][:, :], in_=xq[s * 128:(s + 1) * 128, :]), 16)
            blk.sync(b_load)

            def b_bn(vector, s=s, b=b):
                wait_bar(vector, tA_bar)
                S.wait(vector, tkB_x[s])
                if s >= 4:
                    S.wait(vector, tkB_xn[s - 4])
                st_lnB = st_lnB_r[b]
                for q4 in range(4):
                    S.inc("ve", vector.bn_stats(st_lnB[:, 8 + q4 * 6:8 + (q4 + 1) * 6],
                                                x_slB[b][:, q4 * 512:(q4 + 1) * 512]))
                S.wait(vector, S.tok("ve"))
                S.inc("ve", vector.bn_aggr(st_lnB[:, 2:4], st_lnB[:, 8:32]))
            blk.vector(b_bn)
            tv_var = S.tok("ve")

            def b_rstd(scalar, s=s, b=b):
                S.wait(scalar, tv_var)
                st_lnB = st_lnB_r[b]
                S.inc("sc", scalar.activation(st_lnB[:, 3:4], st_lnB[:, 3:4], AF.Ln, bias=c_eps[:, 0:1]))
                S.wait(scalar, S.tok("sc"))
                S.inc("sc", scalar.activation(st_lnB[:, 3:4], st_lnB[:, 3:4], AF.Exp, scale=c_nh[:, 0:1]))
            blk.scalar(b_rstd)
            ts_rstd = S.tok("sc")

            def b_xn(vector, s=s, b=b):
                S.wait(vector, ts_rstd)
                if s >= 4:
                    S.wait(vector, tkB_q[s - 4])
                tkB_xn[s] = S.inc("ve", vector.tensor_scalar(
                    xn_slB[b][:, :], x_slB[b][:, :], st_lnB_r[b][:, 2:3], st_lnB_r[b][:, 3:4],
                    op0=ALU.subtract, op1=ALU.mult))
            blk.vector(b_xn)

            def b_tr(tensor, s=s, b=b):
                wait_bar(tensor, tA_bar)
                S.wait(tensor, tkB_xn[s], t_ident)
                if s >= 1:
                    S.wait(tensor, tkB_q[s - 1])
                for g in range(2):
                    last = None
                    for i in range(8):
                        k = g * 8 + i
                        last = tensor.transpose(tr_psB[g][:, i * 128:(i + 1) * 128],
                                                xn_slB[b][:, k * 128:(k + 1) * 128], idb[:, :])
                    S.inc("mm", last)
            blk.tensor(b_tr)
            t_tr = S.tok("mm")

            def b_trev(vector, s=s):
                S.wait(vector, ("mm", t_tr[1] - 1))
                xqv = xn_q[:, :].rearrange("p (k c) -> p k c", k=16)
                S.inc("ve", vector.tensor_copy(xqv[:, 0:8, s * 128:(s + 1) * 128],
                                               tr_psB[0][:, :].rearrange("p (i c) -> p i c", i=8)))
                S.wait(vector, t_tr)
                tkB_q[s] = S.inc("ve", vector.tensor_copy(
                    xqv[:, 8:16, s * 128:(s + 1) * 128],
                    tr_psB[1][:, :].rearrange("p (i c) -> p i c", i=8)))
            blk.vector(b_trev)
        t_xnq = tkB_q[NQT - 1]
        tB1_bar = barrier_toks()
        b1.close()

        b2 = ExitStack()
        wq_bf = [SB(b2, f"wq_bf{i}", [128, 16 * 512], BF16) for i in range(2)]
        rt_bf = SB(b2, "rt_bf", [128, 256], BF16)
        q_tm = [SB(b2, f"q_tm{i}", [128, 512], F32) for i in range(4)]
        qrot = [SB(b2, f"qrot{i}", [128, 512], BF16) for i in range(4)]
        rtabB = [SB(b2, f"rtabB{i}", [128, 512], F32) for i in range(4)]
        rtvB = [SB(b2, f"rtvB{i}", [128, 256], F32) for i in range(2)]
        rtgB = [SB(b2, f"rtgB{i}", [128, 256], F32) for i in range(2)]
        qsp_st = [SB(b2, f"qsp_st{i}", [128, 512], BF16) for i in range(4)]
        r_ex = SB(b2, "r_ex", [128, 16], F32)
        r_st = SB(b2, "r_st", [128, 8], F32)
        r_ge = SB(b2, "r_ge", [128, 16], F32)
        r_cb = SB(b2, "r_cb", [128, 16], F32)
        exc_f = SB(b2, "exc_f", [128, 4 * 128], F32)
        excb_bf = SB(b2, "excb_bf", [128, 4 * 128], BF16)
        nrow_bf = SB(b2, "nrow_bf", [128, 4 * 16], BF16)

        Q_ps = [PS(b2, f"Q_ps{i}", [128, 512], F32) for i in range(2)]
        R_ps = PS(b2, "R_ps", [128, 16], F32)
        qtr_ps = [PS(b2, f"qtr_ps{i}", [128, 512], BF16) for i in range(2)]
        nm_ps = PS(b2, "nm_ps", [16, 128], F32)
        bc_psB = PS(b2, "bc_psB", [128, 512], F32)

        def b_rt(gpsimd):
            wait_bar(gpsimd, tB1_bar)
            S.inc("lw", gpsimd.dma_start(out=rt_bf[:, :].rearrange("p (k e) -> p k e", k=16),
                                         in_=rt_r[:, :, :]), 16)
        blk.gpsimd(b_rt)
        t_rt = S.tok("lw")

        tk_wq = [None] * 4
        tk_qmm = [[None] * NQT for _ in range(4)]
        tk_qev = [[None] * NQT for _ in range(4)]
        tk_qrot = [[None] * NQT for _ in range(4)]
        tk_qn = [[None] * NQT for _ in range(4)]
        tk_qtr = [[None] * NQT for _ in range(4)]
        tk_qst = [[None] * NQT for _ in range(4)]
        tk_qsp = [[None] * NQT for _ in range(4)]
        tk_rtb = [[None] * NQT for _ in range(4)]
        tk_qrv = [[None] * NQT for _ in range(4)]
        tk_qsqd = [[None] * NQT for _ in range(4)]
        tk_comb = [None] * NQT
        tk_rmm = [None] * NQT
        tk_rex = [None] * NQT

        for ct in range(4):
            def b_wq(gpsimd, ct=ct):
                wait_bar(gpsimd, tB1_bar)
                if ct >= 2:
                    S.wait(gpsimd, tk_qmm[ct - 2][NQT - 1])
                tk_wq[ct] = S.inc("lw", gpsimd.dma_start(
                    out=wq_bf[ct % 2][:, :].rearrange("p (k f) -> p k f", k=16),
                    in_=wq_r[ct, :, :, :]), 16)
            blk.gpsimd(b_wq)

            def b2_stage1(ct, s):
                b = (ct * NQT + s) % 4
                it = ct * NQT + s
                def b_qmm(tensor, ct=ct, s=s, b=b, it=it):
                    S.wait(tensor, tk_wq[ct], t_xnq)
                    if ct == 0:
                        S.wait(tensor, t_rt)
                        if s >= 1:
                            S.wait(tensor, tk_rex[s - 1])
                    if it >= 2:
                        pct, ps = divmod(it - 2, NQT)
                        S.wait(tensor, tk_qev[pct][ps])
                    xqv = xn_q[:, :].rearrange("p (k c) -> p k c", k=16)
                    last = lr = None
                    for k in range(16):
                        last = tensor.matmul(Q_ps[it % 2][:, :], xqv[:, k, s * 128:(s + 1) * 128],
                                             wq_bf[ct % 2][:, k * 512:(k + 1) * 512],
                                             start=(k == 0), stop=(k == 15))
                        if ct == 0:
                            lr = tensor.matmul(R_ps[:, :], xqv[:, k, s * 128:(s + 1) * 128],
                                               rt_bf[:, k * 16:(k + 1) * 16],
                                               start=(k == 0), stop=(k == 15))
                    if ct == 0:
                        tk_rmm[s] = S.inc("mm", lr)
                    tk_qmm[ct][s] = S.inc("mm", last)
                blk.tensor(b_qmm)

                def b_qev(vector, ct=ct, s=s, b=b, it=it):
                    S.wait(vector, tk_qmm[ct][s])
                    if it >= 4:
                        pct, ps = divmod(it - 4, NQT)
                        S.wait(vector, tk_qrot[pct][ps], tk_qn[pct][ps], tk_qrv[pct][ps])
                    tk_qev[ct][s] = S.inc("ve", vector.tensor_copy(q_tm[b][:, :], Q_ps[it % 2][:, :]))
                blk.vector(b_qev)

                def b_rtabf(sync, ct=ct, s=s, b=b, it=it):
                    wait_bar(sync, tB1_bar)
                    if it >= 4:
                        pct, ps = divmod(it - 4, NQT)
                        S.wait(sync, tk_qrot[pct][ps], tk_qrv[pct][ps])
                    S.inc("ld", sync.dma_start(
                        out=rtabB[b][:, 0:256],
                        in_=cosq[s * 128:(s + 1) * 128, ct * 256:(ct + 1) * 256]), 16)
                    tk_rtb[ct][s] = S.inc("ld", sync.dma_start(
                        out=rtabB[b][:, 256:512],
                        in_=sinq[s * 128:(s + 1) * 128, ct * 256:(ct + 1) * 256]), 16)
                blk.sync(b_rtabf)

                def b_rv(vector, ct=ct, s=s, b=b, it=it):
                    S.wait(vector, tk_qev[ct][s], tk_rtb[ct][s])
                    if it >= 4:
                        pct, ps = divmod(it - 4, NQT)
                        S.wait(vector, tk_qtr[pct][ps], tk_qsqd[pct][ps])
                    qt = q_tm[b][:, :].rearrange("p (h i two) -> p h i two", two=2, i=32)
                    qr = qrot[b][:, :].rearrange("p (h i two) -> p h i two", two=2, i=32)
                    cosv = rtabB[b][:, 0:256].rearrange("p (h i) -> p h i", i=32)
                    sinv = rtabB[b][:, 256:512].rearrange("p (h i) -> p h i", i=32)
                    r0 = rtvB[0][:, :].rearrange("p (h i) -> p h i", i=32)
                    r1 = rtvB[1][:, :].rearrange("p (h i) -> p h i", i=32)
                    S.inc("ve", vector.tensor_tensor(r0, qt[:, :, :, 1], sinv, ALU.mult))
                    S.inc("ve", vector.tensor_tensor(r1, qt[:, :, :, 0], cosv, ALU.mult))
                    S.wait(vector, S.tok("ve"))
                    S.inc("ve", vector.tensor_tensor(qr[:, :, :, 0], r1, r0, ALU.subtract))
                blk.vector(b_rv)
                tv_re2 = S.tok("ve")
                tk_qrv[ct][s] = tv_re2

                def b_rg(gpsimd, ct=ct, s=s, b=b, it=it):
                    S.wait(gpsimd, tk_qev[ct][s], tk_rtb[ct][s])
                    if it >= 4:
                        pct, ps = divmod(it - 4, NQT)
                        S.wait(gpsimd, tk_qtr[pct][ps], tk_qsqd[pct][ps])
                    qt = q_tm[b][:, :].rearrange("p (h i two) -> p h i two", two=2, i=32)
                    qr = qrot[b][:, :].rearrange("p (h i two) -> p h i two", two=2, i=32)
                    cosv = rtabB[b][:, 0:256].rearrange("p (h i) -> p h i", i=32)
                    sinv = rtabB[b][:, 256:512].rearrange("p (h i) -> p h i", i=32)
                    g0 = rtgB[0][:, :].rearrange("p (h i) -> p h i", i=32)
                    g1 = rtgB[1][:, :].rearrange("p (h i) -> p h i", i=32)
                    S.inc("gp", gpsimd.tensor_tensor(g0, qt[:, :, :, 0], sinv, ALU.mult))
                    S.inc("gp", gpsimd.tensor_tensor(g1, qt[:, :, :, 1], cosv, ALU.mult))
                    S.wait(gpsimd, S.tok("gp"))
                    tk_qrot[ct][s] = S.inc("gp", gpsimd.tensor_tensor(qr[:, :, :, 1], g0, g1, ALU.add))
                blk.gpsimd(b_rg)

                if ct == 0:
                    def b_r1(vector, s=s):
                        S.wait(vector, tk_rmm[s])
                        S.inc("ve", vector.reduce_max(r_st[:, 0:1], R_ps[:, :], axis=AX.X, negate=True))
                    blk.vector(b_r1)
                    tv_rm = S.tok("ve")

                    def b_r2(scalar, s=s):
                        S.wait(scalar, tv_rm)
                        tk_rex[s] = S.inc("sc", scalar.activation(r_ex[:, :], R_ps[:, :], AF.Exp,
                                                                  bias=r_st[:, 0:1]))
                    blk.scalar(b_r2)

                    def b_r3(vector, s=s):
                        S.wait(vector, tk_rex[s])
                        S.inc("ve", vector.reduce_max(r_st[:, 1:2], r_ex[:, :], axis=AX.X))
                        S.wait(vector, S.tok("ve"))
                        S.inc("ve", vector.tensor_single_scalar(r_ge[:, :], r_ex[:, :],
                                                                r_st[:, 1:2], ALU.is_equal))
                        S.wait(vector, S.tok("ve"))
                        S.inc("ve", vector.scalar_tensor_tensor(r_ge[:, :], r_ge[:, :], -1e30,
                                                                r_ex[:, :], op0=ALU.mult, op1=ALU.add))
                        S.wait(vector, S.tok("ve"))
                        S.inc("ve", vector.reduce_max(r_st[:, 2:3], r_ge[:, :], axis=AX.X))
                        S.wait(vector, S.tok("ve"))
                        S.inc("ve", vector.tensor_single_scalar(r_ge[:, :], r_ex[:, :],
                                                                r_st[:, 2:3], ALU.is_ge))
                        S.inc("ve", vector.tensor_tensor(r_st[:, 3:4], r_st[:, 1:2],
                                                         r_st[:, 2:3], ALU.add))
                        S.wait(vector, S.tok("ve"))
                        S.inc("ve", vector.reciprocal(r_st[:, 3:4], r_st[:, 3:4]))
                        S.wait(vector, S.tok("ve"))
                        S.inc("ve", vector.scalar_tensor_tensor(r_cb[:, :], r_ex[:, :], r_st[:, 3:4],
                                                                r_ge[:, :], op0=ALU.mult, op1=ALU.mult))
                    blk.vector(b_r3)
                    tv_cb = S.tok("ve")

                    def b_r4(vector, s=s):
                        S.wait(vector, tv_cb)
                        tk_comb[s] = S.inc("ve", vector.tensor_copy(
                            comb_tok[:, s * 16:(s + 1) * 16], r_cb[:, :]))
                    blk.vector(b_r4)


            def b2_stage2(ct, s):
                b = (ct * NQT + s) % 4
                it = ct * NQT + s
                def b_qsq(vector, ct=ct, s=s, b=b, it=it):
                    S.wait(vector, tk_qrot[ct][s], tk_qrv[ct][s])
                    if it >= 1:
                        pct, ps = divmod(it - 1, NQT)
                        S.wait(vector, tk_qn[pct][ps])
                    tk_qsqd[ct][s] = S.inc("ve", vector.tensor_tensor(
                        sqb[:, :], qrot[b][:, :], qrot[b][:, :], ALU.mult))
                blk.vector(b_qsq)
                ts_qsq = S.tok("ve")

                def b_qn(vector, ct=ct, s=s):
                    S.wait(vector, ts_qsq)
                    tk_qn[ct][s] = S.inc("ve", vector.reduce_sum(
                        qn2[:, s * 32 + ct * 8:s * 32 + (ct + 1) * 8],
                        sqb[:, :].rearrange("p (g y) -> p g y", y=64), axis=AX.X))
                blk.vector(b_qn)

                def b_qtr(tensor, ct=ct, s=s, b=b, it=it):
                    S.wait(tensor, tk_qrot[ct][s], tk_qrv[ct][s])
                    if it >= 2:
                        pct, ps = divmod(it - 2, NQT)
                        S.wait(tensor, tk_qst[pct][ps])
                    last = None
                    for t4 in range(4):
                        last = tensor.transpose(qtr_ps[it % 2][:, t4 * 128:(t4 + 1) * 128],
                                                qrot[b][:, t4 * 128:(t4 + 1) * 128], idb[:, :])
                    tk_qtr[ct][s] = S.inc("mm", last)
                blk.tensor(b_qtr)

                def b_qst(vector, ct=ct, s=s, b=b, it=it):
                    S.wait(vector, tk_qtr[ct][s])
                    if it >= 4:
                        pct, ps = divmod(it - 4, NQT)
                        S.wait(vector, tk_qsp[pct][ps])
                    tk_qst[ct][s] = S.inc("ve", vector.tensor_copy(qsp_st[b][:, :], qtr_ps[it % 2][:, :]))
                blk.vector(b_qst)

                def b_qsp(gpsimd, ct=ct, s=s, b=b):
                    S.wait(gpsimd, tk_qst[ct][s])
                    for t4 in range(4):
                        for half64 in range(2):
                            h = ct * 8 + 2 * t4 + half64
                            p_new = (h // 8) * 4 + (h % 4)
                            hi = 0 if (h % 8) < 4 else 1
                            tk_qsp[ct][s] = S.inc("sp", gpsimd.dma_start(
                                out=q_spill[p_new, hi * 64:hi * 64 + 64, s * 128:(s + 1) * 128],
                                in_=qsp_st[b][half64 * 64:half64 * 64 + 64,
                                              t4 * 128:(t4 + 1) * 128]), 16)
                blk.gpsimd(b_qsp)


            for s in range(NQT):
                it0 = ct * NQT + s
                if it0 >= 2:
                    p2 = it0 - 2
                    b2_stage2(p2 // NQT, p2 % NQT)
                b2_stage1(ct, s)
            if ct == 3:
                for tail in [NQT * 4 - 2, NQT * 4 - 1]:
                    b2_stage2(tail // NQT, tail % NQT)

        tk_negrow = [None] * 4
        for j in range(4):
            def b_p1(vector, j=j):
                for s4 in range(4):
                    s = j * 4 + s4
                    S.wait(vector, tk_qn[3][s])
                    S.inc("ve", vector.reduce_sum(
                        qn2p[:, s * 16:(s + 1) * 16].rearrange("p (blk i) -> p blk i", i=4),
                        qn2[:, s * 32:(s + 1) * 32]
                        .rearrange("p (blk two i) -> p blk i two", two=2, i=4),
                        axis=AX.X))
            blk.vector(b_p1)

            for s4 in range(4):
                s = j * 4 + s4

                def b_p2(tensor, j=j, s=s):
                    S.wait(tensor, S.tok("ve"))
                    S.inc("mm", tensor.transpose(nm_ps[:, :], qn2p[:, s * 16:(s + 1) * 16], idf[:, :]))
                blk.tensor(b_p2)
                tmn = S.tok("mm")

                def b_p3(vector, j=j, s4=s4, tmn=tmn):
                    S.wait(vector, tmn)
                    if s4 == 0:
                        S.inc("ve", vector.reduce_max(negp[:, j:j + 1], nm_ps[:, :], axis=AX.X))
                    else:
                        S.inc("ve", vector.reduce_max(r_st[0:16, 4:5], nm_ps[:, :], axis=AX.X))
                        S.wait(vector, S.tok("ve"))
                        S.inc("ve", vector.tensor_max(negp[:, j:j + 1], negp[:, j:j + 1],
                                                      r_st[0:16, 4:5]))
                    S.wait(vector, S.tok("ve"))
                blk.vector(b_p3)

            def b_c8a(vector, j=j):
                S.wait(vector, S.tok("ve"))
                S.inc("ve", vector.tensor_tensor(negp[:, j:j + 1], negp[:, j:j + 1],
                                                 knm2[0:16, 0:1], ALU.mult))
            blk.vector(b_c8a)
            tva = S.tok("ve")

            def b_c8b(scalar, j=j):
                S.wait(scalar, tva)
                S.inc("sc", scalar.activation(negp[:, j:j + 1], negp[:, j:j + 1], AF.Ln))
                S.wait(scalar, S.tok("sc"))
                S.inc("sc", scalar.activation(negp[:, j:j + 1], negp[:, j:j + 1], AF.Exp,
                                              scale=c_hl[0:16, 0:1], bias=c_lb[0:16, 0:1]))
            blk.scalar(b_c8b)
            tsb = S.tok("sc")

            def b_c8c(vector, j=j):
                S.wait(vector, tsb)
                S.inc("ve", vector.tensor_scalar(negp[:, j:j + 1], negp[:, j:j + 1], -1.0, None, ALU.mult))
            blk.vector(b_c8c)
            tvn = S.tok("ve")

            def b_c8d(tensor, j=j):
                S.wait(tensor, tvn)
                S.inc("mm", tensor.transpose(nm_ps[0:1, 0:16], negp[0:16, j:j + 1], idf[0:16, 0:16]))
            blk.tensor(b_c8d)
            tmt = S.tok("mm")

            def b_c8e(vector, j=j):
                S.wait(vector, tmt)
                tk_negrow[j] = S.inc("ve", vector.tensor_copy(negrow[0:1, j * 16:(j + 1) * 16],
                                                              nm_ps[0:1, 0:16]))
            blk.vector(b_c8e)

        def b_xl(sync):
            wait_bar(sync, tB1_bar)
            S.inc("ld", sync.dma_start(out=exc_f[0:4, 0:128], in_=excrow[:, :]), 16)
        blk.sync(b_xl)
        t_excd = S.tok("ld")

        def b_xm(gpsimd):
            S.wait(gpsimd, t_excd)
            for j in range(1, 4):
                S.inc("sp", gpsimd.dma_start(out=exc_f[0:1, j * 128:(j + 1) * 128],
                                             in_=exc_f[j:j + 1, 0:128]), 16)
        blk.gpsimd(b_xm)
        t_excm = S.tok("sp")

        def b_bx(vector):
            for j in range(4):
                S.wait(vector, tk_negrow[j])
            S.wait(vector, t_excm)
            for j in range(4):
                for c8i in range(8):
                    S.inc("ve", vector.tensor_copy(
                        bias_x[0:1, j * 128 + c8i * 16:j * 128 + (c8i + 1) * 16],
                        negrow[0:1, j * 16:(j + 1) * 16]))
            S.wait(vector, S.tok("ve"))
            S.inc("ve", vector.tensor_tensor(bias_x[0:1, :], bias_x[0:1, :], exc_f[0:1, :], ALU.add))
            S.wait(vector, S.tok("ve"))
            S.inc("ve", vector.tensor_copy(excb_bf[0:1, :], bias_x[0:1, :]))
            S.inc("ve", vector.tensor_copy(nrow_bf[0:1, :], negrow[0:1, :]))
        blk.vector(b_bx)
        tv_rows = S.tok("ve")

        def b_bc1(tensor):
            S.wait(tensor, tv_rows)
            S.inc("mm", tensor.matmul(bc_psB[:, 0:64], ones_b[0:1, 0:128], nrow_bf[0:1, :],
                                      start=True, stop=True))
        blk.tensor(b_bc1)
        tm_bc1 = S.tok("mm")

        def b_bcev1(vector):
            S.wait(vector, tm_bc1)
            S.inc("ve", vector.tensor_copy(bias_p[:, :], bc_psB[:, 0:64]))
        blk.vector(b_bcev1)
        tv_bp = S.tok("ve")

        def b_bc2(tensor):
            S.wait(tensor, tv_bp)
            S.inc("mm", tensor.matmul(bc_psB[:, :], ones_b[0:1, 0:128], excb_bf[0:1, :],
                                      start=True, stop=True))
        blk.tensor(b_bc2)
        tm_bc2 = S.tok("mm")

        def b_bcev2(vector):
            S.wait(vector, tm_bc2)
            S.inc("ve", vector.tensor_copy(bias_x[:, :], bc_psB[:, :]))
        blk.vector(b_bcev2)
        tv_bias = S.tok("ve")

        def b_xnsp(gpsimd):
            S.wait(gpsimd, t_xnq)
            S.inc("sp", gpsimd.dma_start(out=xn_spill[:, :], in_=xn_q[:, :]), 16)
        blk.gpsimd(b_xnsp)
        t_xnsp = S.tok("sp")

        tB_bar = barrier_toks()
        b2.close()
        bctx.close()

        cctx = ExitStack()
        Qt = [SB(cctx, f"Qt{i}", [128, 16, 512], BF16) for i in range(2)]
        P_sb = [SB(cctx, f"P_sb{i}", [128, 1024], BF16) for i in range(4)]
        mask_sb = SB(cctx, "mask_sb", [128, 8, 1024], BF16)
        o_st = [SB(cctx, f"o_st{i}", [65, 1024], BF16) for i in range(2)]

        sc_ps = [PS(cctx, f"sc_ps{i}", [128, 1024], F32) for i in range(3)]
        O_psA = PS(cctx, "O_psA", [65, 512], F32)
        O_psB = PS(cctx, "O_psB", [65, 512], F32)

        t_kdup = None

        tk_qt = [None] * 4
        tk_qt_use = [None] * 4
        tk_msk = [None] * 4
        tk_msk_use = [None] * 4
        prev_odone = [None]
        prev_osp = [None]

        for j in range(4):
            Sj = S_STRUCT[j]
            excset = set(EXC_STRUCT[j])
            diagmap = {c: i for i, c in enumerate(DIAG_STRUCT[j])}

            def c_qt(scalar, j=j):
                wait_bar(scalar, tB_bar)
                if j >= 2:
                    S.wait(scalar, tk_qt_use[j - 2])
                for hp in range(16):
                    tk_qt[j] = S.inc("la", scalar.dma_start(
                        out=Qt[j % 2][:, hp, :],
                        in_=q_spill[hp, :, j * 512:(j + 1) * 512]), 16)
            blk.scalar(c_qt)

            def c_msk(sync, j=j):
                wait_bar(sync, tB_bar)
                if j >= 1:
                    S.wait(sync, tk_msk_use[j - 1])
                tk_msk[j] = S.inc("ld", sync.dma_start(
                    out=mask_sb[:, :, :],
                    in_=maskd[j, :, :, :].rearrange("m p c -> p m c")), 16)
            blk.sync(c_msk)

            for hp in range(16):
                kt = hp // 4
                gA, gB = 2 * kt, 2 * kt + 1
                tk_sc = {}
                tk_exp = {}
                tk_mask = {}
                tk_av = {}

                LAG = 2
                for it in range(Sj + LAG):
                    if it < Sj:
                        ci = it
                        pb = ci % 3
                        pp = ci % 4

                        def c_score(tensor, j=j, hp=hp, ci=ci, pb=pb, kt=kt):
                            S.wait(tensor, tk_qt[j], tv_bias)
                            if ci >= 3:
                                S.wait(tensor, tk_exp[ci - 3])
                            tensor.matmul(sc_ps[pb][:, 0:512],
                                          K_cmp[0:64, kt, ci * 128:(ci + 1) * 128],
                                          Qt[j % 2][0:64, hp, :], start=True, stop=True)
                            lB = tensor.matmul(sc_ps[pb][:, 512:1024],
                                               K_cmp[64:128, kt, ci * 128:(ci + 1) * 128],
                                               Qt[j % 2][64:128, hp, :], start=True, stop=True)
                            tk_sc[ci] = S.inc("mm", lB)
                        blk.tensor(c_score)

                        def c_exp(scalar, j=j, hp=hp, ci=ci, pb=pb, pp=pp):
                            S.wait(scalar, tk_sc[ci], t_consts)
                            if ci >= 4:
                                S.wait(scalar, tk_av[ci - 4])
                            if ci in excset:
                                cidx = EXC_STRUCT[j].index(ci)
                                bias_ap = bias_x[:, j * 128 + cidx * 16 + hp:j * 128 + cidx * 16 + hp + 1]
                            else:
                                bias_ap = bias_p[:, j * 16 + hp:j * 16 + hp + 1]
                            tk_exp[ci] = S.inc("sc", scalar.activation(
                                P_sb[pp][:, :], sc_ps[pb][:, :], AF.Exp,
                                bias=bias_ap, scale=c_s8[:, 0:1]))
                        blk.scalar(c_exp)

                        if ci in diagmap:
                            def c_mask(vector, j=j, ci=ci, pp=pp, mi=diagmap[ci]):
                                S.wait(vector, tk_exp[ci], tk_msk[j])
                                tk_mask[ci] = S.inc("ve", vector.tensor_tensor(
                                    P_sb[pp][:, :], P_sb[pp][:, :], mask_sb[:, mi, :], ALU.mult))
                            blk.vector(c_mask)

                    if it >= LAG:
                        ci = it - LAG
                        pp = ci % 4

                        def c_av(tensor, j=j, hp=hp, ci=ci, pp=pp, gA=gA, gB=gB, Sj=Sj):
                            S.wait(tensor, tk_exp[ci])
                            if ci in tk_mask:
                                S.wait(tensor, tk_mask[ci])
                            if ci == 0 and prev_odone[0] is not None:
                                S.wait(tensor, prev_odone[0])
                            tensor.matmul(O_psA[:, :], V_sb[:, ci, gA * 65:(gA + 1) * 65],
                                          P_sb[pp][:, 0:512], start=(ci == 0), stop=(ci == Sj - 1))
                            la = tensor.matmul(O_psB[:, :], V_sb[:, ci, gB * 65:(gB + 1) * 65],
                                               P_sb[pp][:, 512:1024], start=(ci == 0), stop=(ci == Sj - 1))
                            tk_av[ci] = S.inc("mm", la)
                        blk.tensor(c_av)

                t_avl = tk_av[Sj - 1]

                def c_oev(vector, j=j, hp=hp, t_avl=t_avl):
                    S.wait(vector, t_avl)
                    if prev_osp[0] is not None:
                        S.wait(vector, prev_osp[0])
                    S.inc("ve", vector.tensor_copy(o_st[hp % 2][:, 0:512], O_psA[:, :]))
                    prev_odone[0] = S.inc("ve", vector.tensor_copy(
                        o_st[hp % 2][:, 512:1024], O_psB[:, :]))
                blk.vector(c_oev)
                tv_od = S.tok("ve")

                def c_osp(gpsimd, j=j, hp=hp, tv_od=tv_od, kt=kt):
                    S.wait(gpsimd, tv_od)
                    hA = kt * 8 + (hp % 4)
                    hB = hA + 4
                    S.inc("sp", gpsimd.dma_start(
                        out=o_spill[hA // 2, (hA % 2) * 64:(hA % 2) * 64 + 64, j * 512:(j + 1) * 512],
                        in_=o_st[hp % 2][0:64, 0:512]), 16)
                    S.inc("sp", gpsimd.dma_start(
                        out=o_spill[hB // 2, (hB % 2) * 64:(hB % 2) * 64 + 64, j * 512:(j + 1) * 512],
                        in_=o_st[hp % 2][0:64, 512:1024]), 16)
                    S.inc("sp", gpsimd.dma_start(out=z_spill[hA, j * 512:(j + 1) * 512],
                                                 in_=o_st[hp % 2][64:65, 0:512]), 16)
                    prev_osp[0] = S.inc("sp", gpsimd.dma_start(
                        out=z_spill[hB, j * 512:(j + 1) * 512],
                        in_=o_st[hp % 2][64:65, 512:1024]), 16)
                blk.gpsimd(c_osp)

            tk_msk_use[j] = S.tok("ve")
            tk_qt_use[j] = S.tok("mm")

        tC_bar = barrier_toks()
        cctx.close()
        attn_ctx.close()

        dctx = ExitStack()
        y_acc = SB(dctx, "y_acc", [128, 8 * 2048], F32)
        xn_h = SB(dctx, "xn_h", [128, 16 * 1024], BF16)
        gu_bf = [SB(dctx, f"gu_bf{i}", [128, 2048], BF16) for i in range(3)]
        dn_bf = [SB(dctx, f"dn_bf{i}", [128, 6 * 2048], BF16) for i in range(2)]
        H_sb = [SB(dctx, f"H_sb{i}", [128, 6 * 1024], BF16) for i in range(2)]
        z_sb = SB(dctx, "z_sb", [32, 2048], BF16)
        rz_sb = z_sb
        zrow_st = [SB(dctx, f"zrow_st{i}", [128, 1024], BF16) for i in range(4)]

        G_ps = [[PS(dctx, f"G_ps{p}{i}", [128, 512], F32) for i in range(2)]
                for p in range(2)]
        U_ps = [PS(dctx, f"U_ps{i}", [128, 512], F32) for i in range(2)]
        y_ps = [PS(dctx, f"y_ps{i}", [128, 512], F32) for i in range(2)]

        def OH(k, lo, hi):
            if k < 12:
                return dn_bf[0][:, k * 1024 + lo:k * 1024 + hi]
            return dn_bf[1][:, (k - 12) * 1024 + lo:(k - 12) * 1024 + hi]

        def WOS(dt, k):
            if dt % 2 == 0:
                return dn_bf[1][:, 4096 + k * 512:4096 + (k + 1) * 512]
            if k < 12:
                return H_sb[0][:, k * 512:(k + 1) * 512]
            return H_sb[1][:, (k - 12) * 512:(k - 12 + 1) * 512]

        def WOS_full(dt):
            if dt % 2 == 0:
                return [dn_bf[1][:, 4096:12288]]
            return [H_sb[0][:, :], H_sb[1][:, 0:2048]]

        gu_tok, gu_used = {}, {}
        dn_tok, dn_used = {}, {}
        down_done = {}
        down_ya = {}
        tk_gmm, tk_umm = {}, {}
        tk_sil_g, tk_hm_g = {}, {}
        tk_oh = {}
        tk_xq = {}
        tk_zbcmm, tk_zonorm = {}, {}
        t_xnh = [None, None]
        t_xnh_used = [None, None]
        tk_womm_all = [None, None]
        last_ya = [None, None]
        y_store = [[None] * 8, [None] * 8]
        t_rz = [None]

        def emit_gu_load(gt):
            if gt >= 2 * 12 * E:
                return
            ge2, r2 = divmod(gt, 12)
            e2 = ge2 % 16
            fs2, w2 = divmod(r2, 2)
            srcw = gate_r if w2 == 0 else up_r

            def d_gul(gpsimd, gt=gt, e2=e2, fs2=fs2, srcw=srcw):
                if gt >= 3:
                    S.wait(gpsimd, gu_used[gt - 3])
                gu_tok[gt] = S.inc("lw", gpsimd.dma_start(
                    out=gu_bf[gt % 3][:, :].rearrange("p (k f) -> p k f", k=16),
                    in_=srcw[e2, fs2, :, :, :]), 16)
            blk.gpsimd(d_gul)

        def emit_down(half, e):
            ge = half * 16 + e
            for ts in range(8):
                for dt in range(4):
                    par = (ts * 4 + dt) % 2

                    def d_dn(tensor, e=e, ge=ge, ts=ts, dt=dt, par=par):
                        S.wait(tensor, dn_tok[ge], tk_hm_g[ge * 6 + 5])
                        if last_ya[par] is not None:
                            S.wait(tensor, last_ya[par])
                        last = None
                        for fs in range(6):
                            last = tensor.matmul(
                                y_ps[par][:, :],
                                H_sb[e % 2][:, fs * 1024 + ts * 128:fs * 1024 + (ts + 1) * 128],
                                dn_bf[e % 2][:, fs * 2048 + dt * 512:fs * 2048 + (dt + 1) * 512],
                                start=(fs == 0), stop=(fs == 5))
                        S.inc("mm", last)
                    blk.tensor(d_dn)
                    tm_dn = S.tok("mm")

                    def d_ya(vector, half=half, e=e, ge=ge, ts=ts, dt=dt, par=par, tm_dn=tm_dn):
                        S.wait(vector, tm_dn, tk_xq[(half, ts)])
                        sl = y_acc[:, ts * 2048 + dt * 512:ts * 2048 + (dt + 1) * 512]
                        cix = (half * 8 + ts) * 16 + e
                        last_ya[par] = S.inc("ve", vector.scalar_tensor_tensor(
                            sl, y_ps[par][:, :], comb_tok[:, cix:cix + 1], sl,
                            op0=ALU.mult, op1=ALU.add))
                        down_ya[(ge, ts)] = last_ya[par]
                    blk.vector(d_ya)
            dn_used[ge] = S.tok("mm")
            down_done[ge] = dn_used[ge]

        PARANOID = True
        for half in range(2):
            if PARANOID and half == 1:
                tD0_bar = barrier_toks()

                def d_pb(eng):
                    wait_bar(eng, tD0_bar)
                blk.sync(d_pb)
                blk.vector(d_pb)
                blk.scalar(d_pb)
                blk.tensor(d_pb)
                blk.gpsimd(d_pb)

            def d_laload(scalar, half=half):
                if half == 0:
                    wait_bar(scalar, tC_bar)
                    S.wait(scalar, t_xnsp)
                    S.inc("la", scalar.dma_start(out=z_sb[:, :], in_=z_spill[:, :]), 16)
                t_z = S.tok("la")
                if half == 1:
                    S.wait(scalar, down_done[14])
                for k in range(12):
                    tk_oh[(half, k)] = S.inc("la", scalar.dma_start(
                        out=OH(k, 0, 1024), in_=o_spill[k, :, half * 1024:(half + 1) * 1024]), 16)
                if half == 1:
                    S.wait(scalar, t_xnh_used[0])
                for k in range(8):
                    S.inc("la", scalar.dma_start(
                        out=xn_h[:, k * 1024:(k + 1) * 1024],
                        in_=xn_spill[:, k * 2048 + half * 1024:k * 2048 + (half + 1) * 1024]), 16)
                if half == 1:
                    S.wait(scalar, down_done[15])
                for k in range(12, 16):
                    tk_oh[(half, k)] = S.inc("la", scalar.dma_start(
                        out=OH(k, 0, 1024), in_=o_spill[k, :, half * 1024:(half + 1) * 1024]), 16)
                return t_z
            t_z = [None]
            def _cap(scalar, half=half):
                t_z[0] = d_laload(scalar)
            blk.scalar(_cap)
            t_la_xn = S.tok("la")

            if half == 0:
                def d_rz(vector):
                    S.wait(vector, t_z[0])
                    with nc.allow_low_precision("1/Z kept in bf16 by design"):
                        S.inc("ve", vector.reciprocal(rz_sb[:, :], z_sb[:, :]))
                blk.vector(d_rz)
                tv_rz = S.tok("ve")

                def d_zstore(scalar):
                    S.wait(scalar, tv_rz)
                    S.inc("la", scalar.dma_start(out=z_rec[:, :], in_=rz_sb[:, :]), 16)
                blk.scalar(d_zstore)
                t_zst2 = S.tok("la")

                def d_zreload(sync):
                    S.wait(sync, t_zst2)
                    t_rz[0] = S.tok("la")
                blk.sync(d_zreload)

            tk_wo = [None] * 4
            tk_womm = {}

            def d_wol_early(gpsimd, half=half):
                for dt in range(2):
                    if half == 0:
                        wait_bar(gpsimd, tC_bar)
                    else:
                        S.wait(gpsimd, down_done[half * 16 - 1])
                    pieces = WOS_full(dt)
                    if len(pieces) == 1:
                        tk_wo[dt] = S.inc("lw", gpsimd.dma_start(
                            out=pieces[0].rearrange("p (k f) -> p k f", k=16),
                            in_=wo_r[dt, :, :, :]), 16)
                    else:
                        S.inc("lw", gpsimd.dma_start(
                            out=pieces[0].rearrange("p (k f) -> p k f", k=12),
                            in_=wo_r[dt, :, 0:12, :]), 16)
                        tk_wo[dt] = S.inc("lw", gpsimd.dma_start(
                            out=pieces[1].rearrange("p (k f) -> p k f", k=4),
                            in_=wo_r[dt, :, 12:16, :]), 16)
            blk.gpsimd(d_wol_early)

            def d_ld_xn(sync, half=half):
                if half == 1:
                    S.wait(sync, t_xnh_used[0])
                for k in range(8, 16):
                    S.inc("ld", sync.dma_start(
                        out=xn_h[:, k * 1024:(k + 1) * 1024],
                        in_=xn_spill[:, k * 2048 + half * 1024:k * 2048 + (half + 1) * 1024]), 16)

            t_xn_ld = [None]
            if half == 1:
                def _c1(sync):
                    d_ld_xn(sync)
                    t_xn_ld[0] = S.tok("ld")
                blk.sync(_c1)

            for hp in range(16):
                ghp = half * 16 + hp
                p = hp % 2
                rp = hp % 4

                def d_zrow(sync, half=half, hp=hp, ghp=ghp, rp=rp):
                    S.wait(sync, t_rz[0])
                    if half == 0 and hp == 0:
                        wait_bar(sync, tC_bar)
                    if ghp >= 4:
                        S.wait(sync, tk_zbcmm[ghp - 4])
                    S.inc("ld", sync.dma_start(
                        out=zrow_st[rp][0:1, :],
                        in_=z_rec[2 * hp:2 * hp + 1, half * 1024:(half + 1) * 1024]), 16)
                    S.inc("ld", sync.dma_start(
                        out=zrow_st[rp][64:65, :],
                        in_=z_rec[2 * hp + 1:2 * hp + 2, half * 1024:(half + 1) * 1024]), 16)
                blk.sync(d_zrow)
                t_zrow = S.tok("ld")

                def d_zbc(tensor, half=half, hp=hp, ghp=ghp, p=p, rp=rp, t_zrow=t_zrow):
                    if half == 0 and hp == 0:
                        wait_bar(tensor, tC_bar)
                    S.wait(tensor, t_zrow)
                    if half == 1 and hp < 2:
                        S.wait(tensor, tk_sil_g[16 * 6 - 2 + hp])
                    if ghp >= 2:
                        S.wait(tensor, tk_zonorm[ghp - 2])
                    tensor.matmul(G_ps[p][0][0:64, :], ones_b[0:1, 0:64],
                                  zrow_st[rp][0:1, 0:512], start=True, stop=True)
                    tensor.matmul(G_ps[p][1][0:64, :], ones_b[0:1, 0:64],
                                  zrow_st[rp][0:1, 512:1024], start=True, stop=True)
                    tensor.matmul(G_ps[p][0][64:128, :], ones_b[64:65, 0:64],
                                  zrow_st[rp][64:65, 0:512], start=True, stop=True)
                    la = tensor.matmul(G_ps[p][1][64:128, :], ones_b[64:65, 0:64],
                                       zrow_st[rp][64:65, 512:1024], start=True, stop=True)
                    tk_zbcmm[ghp] = S.inc("mm", la)
                blk.tensor(d_zbc)

                def d_onorm(vector, half=half, hp=hp, ghp=ghp, p=p):
                    S.wait(vector, tk_zbcmm[ghp], tk_oh[(half, hp)])
                    with nc.allow_low_precision("O scaled by bf16 1/Z by design"):
                        for tt in range(2):
                            tk_zonorm[ghp] = S.inc("ve", vector.tensor_tensor(
                                OH(hp, tt * 512, (tt + 1) * 512),
                                OH(hp, tt * 512, (tt + 1) * 512),
                                G_ps[p][tt][:, :], ALU.mult))
                blk.vector(d_onorm)

            def d_ld_xq(sync, half=half):
                for ts in range(8):
                    if half == 1:
                        S.wait(sync, y_store[0][ts])
                    tk_xq[(half, ts)] = S.inc("ld", sync.dma_start(
                        out=y_acc[:, ts * 2048:(ts + 1) * 2048],
                        in_=xq[half * 1024 + ts * 128:half * 1024 + (ts + 1) * 128, :]), 16)
                if half == 0:
                    d_ld_xn(sync)
                    t_xn_ld[0] = S.tok("ld")
            blk.sync(d_ld_xq)
            t_xnh[half] = (t_la_xn, t_xn_ld[0])

            for dt in range(4):
                if dt >= 2:
                    def d_wol(gpsimd, half=half, dt=dt):
                        S.wait(gpsimd, tk_womm[(dt - 2, 7)])
                        pieces = WOS_full(dt)
                        if len(pieces) == 1:
                            tk_wo[dt] = S.inc("lw", gpsimd.dma_start(
                                out=pieces[0].rearrange("p (k f) -> p k f", k=16),
                                in_=wo_r[dt, :, :, :]), 16)
                        else:
                            S.inc("lw", gpsimd.dma_start(
                                out=pieces[0].rearrange("p (k f) -> p k f", k=12),
                                in_=wo_r[dt, :, 0:12, :]), 16)
                            tk_wo[dt] = S.inc("lw", gpsimd.dma_start(
                                out=pieces[1].rearrange("p (k f) -> p k f", k=4),
                                in_=wo_r[dt, :, 12:16, :]), 16)
                    blk.gpsimd(d_wol)

                for ts in range(8):
                    par = (dt * 8 + ts) % 2

                    def d_wom(tensor, half=half, dt=dt, ts=ts, par=par):
                        S.wait(tensor, tk_wo[dt])
                        if last_ya[par] is not None:
                            S.wait(tensor, last_ya[par])
                        last = None
                        for k in range(16):
                            S.wait(tensor, tk_zonorm[half * 16 + k], tk_oh[(half, k)])
                            last = tensor.matmul(
                                y_ps[par][:, :],
                                OH(k, ts * 128, (ts + 1) * 128),
                                WOS(dt, k), start=(k == 0), stop=(k == 15))
                        tk_womm[(dt, ts)] = S.inc("mm", last)
                    blk.tensor(d_wom)

                    def d_woya(vector, half=half, dt=dt, ts=ts, par=par):
                        S.wait(vector, tk_womm[(dt, ts)], tk_xq[(half, ts)])
                        sl = y_acc[:, ts * 2048 + dt * 512:ts * 2048 + (dt + 1) * 512]
                        last_ya[par] = S.inc("ve", vector.tensor_tensor(
                            sl, sl, y_ps[par][:, :], ALU.add))
                    blk.vector(d_woya)
            tk_womm_all[half] = S.tok("mm")

            if half == 0:
                for gt0 in range(3):
                    emit_gu_load(gt0)

            for e in range(E):
                ge = half * 16 + e
                eb = e % 2

                def d_we(gpsimd, half=half, e=e, eb=eb, ge=ge):
                    if e < 2:
                        S.wait(gpsimd, tk_womm_all[half])
                    else:
                        S.wait(gpsimd, dn_used[ge - 2])
                    last = None
                    for fs in range(6):
                        last = S.inc("lw", gpsimd.dma_start(
                            out=dn_bf[eb][:, fs * 2048:(fs + 1) * 2048],
                            in_=down_r[e, fs * 128:(fs + 1) * 128, :]), 16)
                    dn_tok[ge] = last
                blk.gpsimd(d_we)

                for fs in range(6):
                    gfs = ge * 6 + fs
                    gti = ge * 12 + fs * 2
                    uti = gti + 1

                    def d_g(tensor, half=half, e=e, fs=fs, gfs=gfs, gti=gti):
                        S.wait(tensor, gu_tok[gti], *t_xnh[half])
                        if e == 0 and fs < 2:
                            S.wait(tensor, tk_zonorm[half * 16 + 14 + fs])
                        else:
                            S.wait(tensor, tk_sil_g[gfs - 2])
                        Gp = G_ps[fs % 2]
                        last = None
                        for k in range(16):
                            lhs = gu_bf[gti % 3][:, k * 128:(k + 1) * 128]
                            tensor.matmul(Gp[0][:, :], lhs, xn_h[:, k * 1024:k * 1024 + 512],
                                          start=(k == 0), stop=(k == 15))
                            last = tensor.matmul(Gp[1][:, :], lhs,
                                                 xn_h[:, k * 1024 + 512:(k + 1) * 1024],
                                                 start=(k == 0), stop=(k == 15))
                        tk_gmm[gfs] = S.inc("mm", last)
                        gu_used[gti] = tk_gmm[gfs]
                    blk.tensor(d_g)
                    emit_gu_load(gti + 3)

                    def d_u(tensor, half=half, e=e, fs=fs, gfs=gfs, uti=uti):
                        S.wait(tensor, gu_tok[uti])
                        if gfs >= 1:
                            S.wait(tensor, tk_hm_g[gfs - 1])
                        last = None
                        for k in range(16):
                            lhs = gu_bf[uti % 3][:, k * 128:(k + 1) * 128]
                            tensor.matmul(U_ps[0][:, :], lhs, xn_h[:, k * 1024:k * 1024 + 512],
                                          start=(k == 0), stop=(k == 15))
                            last = tensor.matmul(U_ps[1][:, :], lhs,
                                                 xn_h[:, k * 1024 + 512:(k + 1) * 1024],
                                                 start=(k == 0), stop=(k == 15))
                        tk_umm[gfs] = S.inc("mm", last)
                        gu_used[uti] = tk_umm[gfs]
                        if fs == 5:
                            t_xnh_used[half] = tk_umm[gfs]
                    blk.tensor(d_u)
                    emit_gu_load(uti + 3)

                    def d_sil(scalar, half=half, e=e, eb=eb, ge=ge, fs=fs, gfs=gfs):
                        S.wait(scalar, tk_gmm[gfs])
                        if e < 2:
                            S.wait(scalar, tk_womm_all[half])
                        if ge >= 2:
                            S.wait(scalar, down_done[ge - 2])
                        with nc.allow_low_precision("bf16 h-path by design"):
                            S.inc("sc", scalar.activation(
                                H_sb[eb][:, fs * 1024:fs * 1024 + 512],
                                G_ps[fs % 2][0][:, :], AF.Silu))
                            tk_sil_g[gfs] = S.inc("sc", scalar.activation(
                                H_sb[eb][:, fs * 1024 + 512:(fs + 1) * 1024],
                                G_ps[fs % 2][1][:, :], AF.Silu))
                    blk.scalar(d_sil)

                    for tt in range(2):
                        def d_hm(vector, eb=eb, fs=fs, gfs=gfs, tt=tt):
                            S.wait(vector, tk_sil_g[gfs], tk_umm[gfs])
                            sl = H_sb[eb][:, fs * 1024 + tt * 512:fs * 1024 + (tt + 1) * 512]
                            with nc.allow_low_precision("bf16 h-path by design"):
                                tk = S.inc("ve", vector.tensor_tensor(
                                    sl, sl, U_ps[tt][:, :], ALU.mult))
                            if tt == 1:
                                tk_hm_g[gfs] = tk
                        blk.vector(d_hm)

                if e >= 1:
                    emit_down(half, e - 1)
            emit_down(half, E - 1)

            for ts in range(8):
                def d_st(scalar, half=half, ts=ts):
                    S.wait(scalar, down_ya[(half * 16 + E - 1, ts)])
                    y_store[half][ts] = S.inc("so", scalar.dma_start(
                        out=out[half * 1024 + ts * 128:half * 1024 + (ts + 1) * 128, :],
                        in_=y_acc[:, ts * 2048:(ts + 1) * 2048]), 16)
                blk.scalar(d_st)

        def fin(scalar):
            S.wait(scalar, S.tok("so"))
        blk.scalar(fin)
        dctx.close()

    with nc.Block() as _blk:
        _blk.sync(lambda e: _emit("sync", e))
        _blk.vector(lambda e: _emit("vector", e))
        _blk.scalar(lambda e: _emit("scalar", e))
        _blk.tensor(lambda e: _emit("tensor", e))
        _blk.gpsimd(lambda e: _emit("gpsimd", e))

    ctx.close()
    return nc


_CACHE = {}


def _host_prep(L0_attn_x, L0_attn_wq, L0_attn_wk, L0_attn_wv, L0_attn_wo,
               L0_ffn_router_w, L0_ffn_gate_w, L0_ffn_up_w, L0_ffn_down_w):
    f32 = np.float32
    wq = np.asarray(L0_attn_wq, f32)
    wk = np.asarray(L0_attn_wk, f32)
    wv = np.asarray(L0_attn_wv, f32)
    wo = np.asarray(L0_attn_wo, f32)
    rw = np.asarray(L0_ffn_router_w, f32)
    gw = np.asarray(L0_ffn_gate_w, f32)
    uw = np.asarray(L0_ffn_up_w, f32)
    dw = np.asarray(L0_ffn_down_w, f32)
    x = np.asarray(L0_attn_x, f32)

    wkT = np.ascontiguousarray(wk.T)
    wvT = np.ascontiguousarray(wv.T)
    wqT = np.ascontiguousarray(wq.T)
    woT = np.ascontiguousarray(wo.T)
    wk_r = np.ascontiguousarray(wkT.reshape(16, 128, 512).transpose(1, 0, 2))
    wv_r = np.ascontiguousarray(wvT.reshape(16, 128, 512).transpose(1, 0, 2))
    wq_r = np.ascontiguousarray(wqT.reshape(16, 128, 4, 512).transpose(2, 1, 0, 3))
    wo_r = np.ascontiguousarray(woT.reshape(16, 128, 4, 512).transpose(2, 1, 0, 3))
    rt_r = np.ascontiguousarray(rw.reshape(16, 128, 16).transpose(1, 0, 2))
    gate_r = np.ascontiguousarray(
        gw.reshape(E, 16, 128, 6, 128).transpose(0, 3, 2, 1, 4))
    up_r = np.ascontiguousarray(
        uw.reshape(E, 16, 128, 6, 128).transpose(0, 3, 2, 1, 4))
    down_r = np.ascontiguousarray(dw)

    i32 = np.arange(0, HD, 2, dtype=f32) / np.float32(HD)
    theta = (1.0 / (np.float32(10000.0) ** i32)).astype(f32)
    tpos = np.arange(T, dtype=f32)
    freqs = (tpos[:, None] * theta[None, :]).astype(f32)
    cosf = np.cos(freqs).astype(f32)
    sinf = np.sin(freqs).astype(f32)
    cosk = np.ascontiguousarray(np.tile(cosf, (1, HKV)))
    sink = np.ascontiguousarray(np.tile(sinf, (1, HKV)))

    identb = np.eye(128, dtype=BFd)
    identf = np.eye(128, dtype=f32)

    common = dict(wk_r=wk_r, wv_r=wv_r, wq_r=wq_r, wo_r=wo_r, rt_r=rt_r,
                  gate_r=gate_r, up_r=up_r, down_r=down_r,
                  cosk=cosk, sink=sink, identb=identb, identf=identf)

    in_maps = []
    pos_locals = []
    for c in range(8):
        seq, half = c // 2, c % 2
        starts = TILE_STARTS[half]
        pos_local = np.concatenate([np.arange(s0, s0 + 512) for s0 in starts])
        pos_locals.append(pos_local)
        xs = np.ascontiguousarray(x[seq])
        xqs = np.ascontiguousarray(x[seq][pos_local])
        cosq = np.ascontiguousarray(np.tile(cosf[pos_local], (1, H)))
        sinq = np.ascontiguousarray(np.tile(sinf[pos_local], (1, H)))
        ncc = [(s0 + 512) // 128 for s0 in starts]
        maskd = np.ones((4, 8, 128, 1024), dtype=BFd)
        for j in range(4):
            t0 = starts[j]
            for mi, cch in enumerate(DIAG_STRUCT[j]):
                if cch >= ncc[j]:
                    continue
                keys = cch * 128 + np.arange(128)[:, None]
                qs = t0 + np.arange(512)[None, :]
                m = (keys <= qs).astype(np.float32).astype(BFd)
                maskd[j, mi, :, 0:512] = m
                maskd[j, mi, :, 512:1024] = m
        excrow = np.zeros((4, 128), dtype=f32)
        for j in range(4):
            for ci, cch in enumerate(EXC_STRUCT[j]):
                if cch >= ncc[j]:
                    excrow[j, ci * 16:(ci + 1) * 16] = -1e30
        m = dict(common)
        m.update(x=xs, xq=xqs, cosq=cosq, sinq=sinq,
                 maskd=np.ascontiguousarray(maskd),
                 excrow=np.ascontiguousarray(excrow))
        in_maps.append(m)
    return in_maps, pos_locals


def kernel(L0_attn_x, L0_attn_wq, L0_attn_wk, L0_attn_wv, L0_attn_wo,
           L0_ffn_router_w, L0_ffn_gate_w, L0_ffn_up_w, L0_ffn_down_w):
    in_maps, pos_locals = _host_prep(
        L0_attn_x, L0_attn_wq, L0_attn_wk, L0_attn_wv, L0_attn_wo,
        L0_ffn_router_w, L0_ffn_gate_w, L0_ffn_up_w, L0_ffn_down_w)
    if "nc" not in _CACHE:
        _CACHE["nc"] = build_nc()
    nc = _CACHE["nc"]
    res = run_bass_kernel_spmd(nc, in_maps, core_ids=list(range(8)))
    _CACHE["res"] = res
    y = np.empty((B, T, D), dtype=np.float32)
    for c in range(8):
        seq = c // 2
        y[seq, pos_locals[c], :] = np.asarray(res.results[c]["out"], dtype=np.float32)
    return y
